# version 32
# baseline (speedup 1.0000x reference)
import numpy as np
import ml_dtypes
from contextlib import ExitStack

import concourse.bass as bass
import concourse.mybir as mybir
from concourse.bass_utils import run_bass_kernel_spmd

F32 = mybir.dt.float32
BF16 = mybir.dt.bfloat16
AF = mybir.ActivationFunctionType
ALU = mybir.AluOpType
AX = mybir.AxisListType

S = 4096
D = 1024
H = 16
FF = 4096
NT = S // 128
NCH = S // 512
EPS_LN = 1e-5
EPS_RMS = 1e-8
ENGS = ("t", "v", "a", "g", "y")


class _Eng:

    def __init__(self, prog, name):
        object.__setattr__(self, "prog", prog)
        object.__setattr__(self, "name", name)
        object.__setattr__(self, "real", None)
        object.__setattr__(self, "last", None)

    def __getattr__(self, op):
        def fn(*args, **kw):
            if self.real is not None:
                object.__setattr__(self, "last",
                                   getattr(self.real, op)(*args, **kw))
            return self.last
        return fn

    def mark(self, key, inc=1):
        prog = self.prog
        if prog.planning:
            prog.ctr[self.name] += inc
            assert key not in prog.marks, f"dup mark {key}"
            prog.marks[key] = (self.name, prog.ctr[self.name])
        elif self.real is not None and inc:
            self.last.then_inc(prog.sems[self.name], inc)

    def wait(self, key):
        prog = self.prog
        if prog.planning or key is None or self.real is None:
            return
        src, val = prog.marks[key]
        seen = prog.waited.get((self.name, src), 0)
        if val <= seen:
            return
        prog.waited[(self.name, src)] = val
        self.real.wait_ge(prog.sems[src], val)


DMA_STREAMS = {"c": 1, "wkv": 1, "wq": 1, "wo": 1, "xb": 2, "aw": 2,
               "xb2": 2, "xf": 2, "spill": 2, "w3": 3, "x1b": 2,
               "x1f": 2, "oD": 2}


class _Prog:
    def __init__(self):
        self.marks = {}
        self.ctr = {n: 0 for n in ENGS}
        self.planning = True
        self.sems = {}
        self.waited = {}
        self.snum = {}
        for n in ENGS:
            setattr(self, n, _Eng(self, n))

    def _dma(self, eng, stream, key, out, in_):
        ring = DMA_STREAMS[stream]
        if self.planning:
            o = self.snum.get(stream, 0)
            self.snum[stream] = o + 1
            sname = f"q_{stream}{o % ring}"
            assert key not in self.marks, key
            self.marks[key] = (sname, 16 * (o // ring + 1))
        if eng.real is not None:
            ins = eng.real.dma_start(out, in_)
            object.__setattr__(eng, "last", ins)
            sname, _ = self.marks[key]
            ins.then_inc(self.sems[sname], 16)

    def gdma(self, key, out, in_, stream=None):
        self._dma(self.g, stream or key[0], key, out, in_)

    def ydma(self, key, out, in_, stream=None):
        self._dma(self.y, stream or key[0], key, out, in_)


def _emit(nc, block_name, sched, P):
    P.planning = True
    sched(P)
    P.planning = False
    with nc.Block(block_name) as block:
        def run(live, real):
            for n in ENGS:
                object.__setattr__(getattr(P, n), "real", None)
            object.__setattr__(getattr(P, live), "real", real)
            P.waited = {}
            sched(P)
            object.__setattr__(getattr(P, live), "real", None)
        block.tensor(lambda e: run("t", e))
        block.vector(lambda e: run("v", e))
        block.scalar(lambda e: run("a", e))
        block.gpsimd(lambda e: run("g", e))
        block.sync(lambda e: run("y", e))


def build_program():
    nc = bass.Bass("TRN2", target_bir_lowering=False)

    x_hbm = nc.dram_tensor("x", [S, D], F32, kind="ExternalInput")
    wqkv_hbm = nc.dram_tensor("w_qkv", [D, 3 * D], F32, kind="ExternalInput")
    wout_hbm = nc.dram_tensor("w_out", [D, D], F32, kind="ExternalInput")
    wup_hbm = nc.dram_tensor("w_up", [FF, D], F32, kind="ExternalInput")
    wdn_hbm = nc.dram_tensor("w_down", [D, FF], F32, kind="ExternalInput")
    out_hbm = nc.dram_tensor("out", [S, D], F32, kind="ExternalOutput")
    x1_hbm = nc.dram_tensor("x1spill", [S, D], F32)

    idn_h = nc.inline_tensor(np.eye(128, dtype=ml_dtypes.bfloat16), "idn")
    tri = np.triu(np.ones((128, 128), np.float32)).astype(ml_dtypes.bfloat16)
    tri_h = nc.inline_tensor(tri, "tri2")
    ones_r_h = nc.inline_tensor(np.ones((1, 128), np.float32), "ones_r")
    ones_c_h = nc.inline_tensor(np.ones((128, 1), np.float32), "ones_c")
    half_h = nc.inline_tensor(np.full((128, 1), 0.5, np.float32), "half")
    qpat_np = np.zeros((128, 8, 16), np.float32)
    for ch in range(8):
        qpat_np[0:64, ch, 2 * ch] = 1.0
        qpat_np[64:128, ch, 2 * ch + 1] = 1.0
    qpat_h = nc.inline_tensor(qpat_np.astype(ml_dtypes.bfloat16), "qpat")
    epsln_h = nc.inline_tensor(np.full((128, 1), EPS_LN, np.float32), "epsln")

    P = _Prog()

    with ExitStack() as octx:
        def osb(name, shape, dtype):
            return octx.enter_context(nc.sbuf_tensor(name, shape, dtype))

        for n in ENGS:
            P.sems[n] = octx.enter_context(nc.semaphore("sem_" + n))
        for sname, ring in DMA_STREAMS.items():
            for i in range(ring):
                P.sems[f"q_{sname}{i}"] = octx.enter_context(
                    nc.semaphore(f"q_{sname}{i}"))

        r1_all = osb("r1_all", [128, NT], F32)
        idn = osb("idn_s", [128, 128], BF16)
        ones_r = osb("ones_r_s", [1, 128], F32)
        ones_c = osb("ones_c_s", [128, 1], F32)
        half_s = osb("half_s", [128, 1], F32)
        epsln_s = osb("epsln_s", [128, 1], F32)
        qpat_s = osb("qpat_s", [128, 8, 16], BF16)
        thr4 = osb("thr4", [128, 4], F32)
        sm = osb("sm", [128, 23], F32)
        bns = osb("bns", [128, 14], F32)
        mctx = octx.enter_context(ExitStack())

        def msb(name, shape, dtype):
            return mctx.enter_context(nc.sbuf_tensor(name, shape, dtype))

        KT = msb("KT", [128, 8, S], BF16)
        VS = msb("VS", [128, NT, H, 65], BF16)
        tri_s = msb("tri_s", [128, 128], BF16)
        mk2 = msb("mk2", [1, 1], F32)
        smb = msb("smb", [128, 16], BF16)

        (C_AU, C_AD, C_TOT, C_TOT2, C_MQ, C_C2, C_CS, C_M, C_MSQ, C_VAR,
         C_LV, C_RSTD, C_SS1, C_S1, C_T1, C_YSUM, C_YSS, C_SS2, C_S2,
         C_R2, C_VE, C_SD) = range(22)

        def smc(i, n=1):
            return sm[:, i:i + n]

        with ExitStack() as ctx:
            def sb(name, shape, dtype):
                return ctx.enter_context(nc.sbuf_tensor(name, shape, dtype))

            wkv = sb("wkv", [128, 8, 2 * D], BF16)
            sm16 = sb("sm16", [128, 16], F32)
            xb1 = sb("xb1", [128, 2, D], BF16)
            xTs = sb("xTs", [128, 2, 8, 128], BF16)
            kbuf = sb("kbuf", [128, 2, D], BF16)
            sqK = sb("sqK", [128, D], BF16)
            sqA = sb("sqA", [128, D], BF16)
            wr1 = sb("wr1", [128, 2, D], F32)
            aasU = sb("aasU", [128, 32], F32)
            aasD = sb("aasD", [128, 32], F32)
            kmax = sb("kmax", [128, H], F32)

            kvp = ctx.enter_context(nc.psum_tensor("kvp", [128, 2048], F32))
            trp = ctx.enter_context(nc.psum_tensor("trp", [128, 512], F32))
            trq = ctx.enter_context(nc.psum_tensor("trq", [128, 512], F32))
            tru = ctx.enter_context(nc.psum_tensor("tru", [128, 512], F32))
            trv = ctx.enter_context(nc.psum_tensor("trv", [128, 512], F32))
            trp_b = trp.bitcast(BF16)
            trq_b = trq.bitcast(BF16)
            tru_b = tru.bitcast(BF16)
            trv_b = trv.bitcast(BF16)

            def phase1(E):
                t, v, a, g, y = E.t, E.v, E.a, E.g, E.y
                E.ydma(("c", 0), idn[:], idn_h[:])
                E.ydma(("c", 1), tri_s[:], tri_h[:])
                E.ydma(("c", 2), ones_r[:], ones_r_h[:])
                E.ydma(("c", 3), ones_c[:], ones_c_h[:])
                E.ydma(("c", 4), half_s[:], half_h[:])
                E.ydma(("c", 5), epsln_s[:], epsln_h[:])
                E.ydma(("c", 6), qpat_s[:], qpat_h[:])
                E.gdma(("wkv",), wkv[:],
                       wqkv_hbm[:, D:3 * D].rearrange("(c p) n -> p c n",
                                                      p=128))
                v.memset(kmax[:], 0.0)
                v.mark(("km0",))
                v.memset(VS[:], 1.0)
                v.mark(("vs1",))
                a.wait(("c", 6))
                a.activation(smc(C_SD), ones_c[:], AF.Exp)
                a.mark(("apin",))

                def k_trans(i):
                    t.wait(("ke", i))
                    if i >= 2:
                        t.wait(("kte", i - 2))
                    ktb = (tru_b, trv_b)[i % 2]
                    for c in range(8):
                        t.transpose(ktb[:, 128 * c:128 * c + 128],
                                    kbuf[:, i % 2, 128 * c:128 * c + 128],
                                    idn[:])
                    t.mark(("ktr", i))
                    v.wait(("ktr", i))
                    v.tensor_copy(KT[:, :, 128 * i:128 * i + 128],
                                  ktb[:, 0:1024].rearrange(
                                      "p (c n) -> p c n", n=128))
                    v.mark(("kte", i))

                for i in range(NT):
                    if i >= 2:
                        g.wait(("xT", i - 2))
                    E.gdma(("xb", i), xb1[:, i % 2, :],
                           x_hbm[128 * i:128 * i + 128, :])
                    t.wait(("xb", i))
                    t.wait(("c", 6))
                    if i >= 2:
                        t.wait(("xTe", i - 2))
                    xtb = (trp_b, trq_b)[i % 2]
                    for c in range(8):
                        t.transpose(xtb[:, 128 * c:128 * c + 128],
                                    xb1[:, i % 2, 128 * c:128 * c + 128],
                                    idn[:])
                    t.mark(("xT", i))
                    v.wait(("xT", i))
                    if i >= 2:
                        v.wait(("kv", i - 2))
                    v.tensor_copy(xTs[:, i % 2],
                                  xtb[:, 0:1024].rearrange(
                                      "p (c n) -> p c n", n=128))
                    v.mark(("xTe", i))
                    if i >= 1:
                        k_trans(i - 1)
                    t.wait(("xTe", i))
                    t.wait(("wkv",))
                    for nb in range(4):
                        if nb == 0 and i >= 1:
                            t.wait(("ke", i - 1))
                        if nb == 2 and i >= 1:
                            t.wait(("ve", i - 1))
                        for c in range(8):
                            t.matmul(kvp[:, 512 * nb:512 * nb + 512],
                                     xTs[:, i % 2, c, :],
                                     wkv[:, c, 512 * nb:512 * nb + 512],
                                     start=(c == 0), stop=(c == 7))
                        if nb == 1:
                            t.mark(("kvK", i))
                    t.mark(("kv", i))
                    a.wait(("kvK", i))
                    if i >= 2:
                        a.wait(("ktr", i - 2))
                        a.wait(("knr", i - 2))
                    a.activation(kbuf[:, i % 2, :], kvp[:, 0:D], AF.Copy)
                    a.mark(("ke", i))
                    a.wait(("kv", i))
                    a.wait(("vs1",))
                    a.activation(VS[:, i, :, 0:64],
                                 kvp[:, D:2 * D].rearrange(
                                     "p (h e) -> p h e", e=64), AF.Copy)
                    a.mark(("ve", i))
                    for ii in ([i - 1] if i >= 1 else []) + \
                            ([i] if i == NT - 1 else []):
                        v.wait(("ve", ii))
                        v.scalar_tensor_tensor(sqK[:], kbuf[:, ii % 2, :],
                                               1.0, kbuf[:, ii % 2, :],
                                               ALU.mult, ALU.mult)
                        v.mark(("ksq", ii))
                        v.wait(("ksq", ii))
                        v.reduce_sum(sm16[:], sqK[:].rearrange(
                            "p (h e) -> p h e", e=64), axis=AX.X)
                        v.mark(("knr", ii))
                        v.wait(("knr", ii))
                        v.wait(("km0",))
                        v.tensor_max(kmax[:], kmax[:], sm16[:])
                        v.mark(("kmx", ii))
                    for r in (2 * i, 2 * i + 1):
                        if r >= 2:
                            y.wait(("aact", r - 2))
                        if r < 32:
                            src_ap = wup_hbm[128 * r:128 * r + 128, :]
                        else:
                            rr = r - 32
                            src_ap = wdn_hbm[
                                128 * (rr // 4):128 * (rr // 4) + 128,
                                1024 * (rr % 4):1024 * (rr % 4) + 1024]
                        E.ydma(("aw", r), wr1[:, r % 2, :], src_ap)
                        a.wait(("aw", r))
                        if r >= 1:
                            a.wait(("aact", r - 1))
                        acc = (aasU[:, r:r + 1] if r < 32
                               else aasD[:, r - 32:r - 31])
                        a.activation(sqA[:], wr1[:, r % 2, :], AF.Abs,
                                     accum_out=acc)
                        a.mark(("aact", r))

                k_trans(NT - 1)

                v.wait(("aact", 31))
                v.reduce_sum(smc(C_AU), aasU[:], axis=AX.X)
                v.mark(("redU",))
                v.wait(("aact", 63))
                v.reduce_sum(smc(C_AD), aasD[:], axis=AX.X)
                v.mark(("redD",))
                t.wait(("redU",))
                t.wait(("redD",))
                t.wait(("c", 6))
                t.wait(("ve", NT - 1))
                t.matmul(kvp[0:1, 0:1], smc(C_AU), ones_c[:],
                         start=True, stop=True)
                t.matmul(kvp[0:1, 1:2], smc(C_AD), ones_c[:],
                         start=True, stop=True)
                t.mark(("atot",))
                v.wait(("atot",))
                v.tensor_scalar(sm[0:1, C_TOT:C_TOT + 2], kvp[0:1, 0:2],
                                0.5 / float(FF * D), None, ALU.mult)
                v.mark(("tot01",))
                t.wait(("tot01",))
                t.matmul(kvp[:, 2:4], ones_r[:], sm[0:1, C_TOT:C_TOT + 2],
                         start=True, stop=True)
                t.mark(("thrb",))
                v.wait(("thrb",))
                v.tensor_copy(thr4[:, 0:2], kvp[:, 2:4])
                v.mark(("thr",))
                v.wait(("thr",))
                v.tensor_scalar(thr4[:, 2:4], thr4[:, 0:2], -1.0, None,
                                ALU.mult)
                v.mark(("thrn",))

                v.wait(("kmx", NT - 1))
                v.tensor_copy(smb[:], kmax[:])
                v.mark(("kmb",))
                t.wait(("kmb",))
                t.wait(("xTe", NT - 1))
                t.transpose(trp_b[0:16, 0:128], smb[:], idn[:])
                t.mark(("kmT",))
                v.wait(("kmT",))
                v.reduce_max(sm[0:16, C_MQ:C_MQ + 1], trp_b[0:16, 0:128],
                             axis=AX.X)
                v.mark(("km16",))
                v.wait(("km16",))
                v.tensor_copy(smb[0:16, 0:1], sm[0:16, C_MQ:C_MQ + 1])
                v.mark(("km16b",))
                t.wait(("km16b",))
                t.wait(("kte", NT - 1))
                t.transpose(trq_b[0:1, 0:16], smb[0:16, 0:1], idn[0:16, 0:16])
                t.mark(("kmT2",))
                v.wait(("kmT2",))
                v.reduce_max(mk2[:], trq_b[0:1, 0:16], axis=AX.X)
                v.mark(("mk2",))

            _emit(nc, "phase1", phase1, P)

        nc.all_engine_barrier()

        with ExitStack() as ctx:
            def sb(name, shape, dtype):
                return ctx.enter_context(nc.sbuf_tensor(name, shape, dtype))

            wq = sb("wq", [128, 8, D], BF16)
            wo = sb("wo", [128, 8, D], BF16)
            QT = sb("QT", [128, 9, 512], BF16)
            xb2 = sb("xb2", [128, 2, D], BF16)
            xfy = sb("xfy", [128, 2, D], F32)
            xTs4 = sb("xTs4", [128, 8, 512], BF16)
            aTs = sb("aTs", [128, 2, 8, 128], BF16)
            p_sb = sb("p_sb", [128, 2, 1024], BF16)
            attO = sb("attO", [128, 4, D], BF16)
            recs = sb("recs", [128, 8], F32)
            negC2 = sb("negC2", [128, 2], F32)
            qmax = sb("qmax", [16, 1], F32)

            sc0 = ctx.enter_context(nc.psum_tensor("sc0", [128, 1024], F32))
            sc1 = ctx.enter_context(nc.psum_tensor("sc1", [128, 1024], F32))
            pvA = ctx.enter_context(nc.psum_tensor("pvA", [128, 4, 66], F32))
            pvB = ctx.enter_context(nc.psum_tensor("pvB", [128, 4, 66], F32))
            mmp = ctx.enter_context(nc.psum_tensor("mmp", [128, 1024], F32))
            mmp_b = mmp.bitcast(BF16)
            sc0_b = sc0.bitcast(BF16)
            scs = (sc0, sc1)
            pv_even = ((pvA, 0, 264), (pvB, 0, 264))
            pv_odd = pv_even

            def phase2(E):
                t, v, a, g, y = E.t, E.v, E.a, E.g, E.y
                E.gdma(("wq",), wq[:],
                       wqkv_hbm[:, 0:D].rearrange("(c p) n -> p c n", p=128))
                E.gdma(("wo",), wo[:],
                       wout_hbm.rearrange("(c p) n -> p c n", p=128))

                st2 = {"mm6": None, "mm7": None}
                chunk_order = {}

                def qsl(cc, i):
                    return (8 * cc + i) % 9

                def emit_qtmm(cc, dch):
                    dc, half = dch // 2, dch % 2
                    if half == 0:
                        t.wait(("qxTe", 4 * cc + 3))
                        t.wait(("wq",))
                        t.wait(st2["mm6"])
                        for ch in range(4):
                            t.matmul(mmp[:, 0:512],
                                     wq[:, ch, 128 * dc:128 * dc + 128],
                                     xTs4[:, ch, :],
                                     start=(ch == 0), stop=False)
                        return
                    for ch in range(4, 8):
                        t.matmul(mmp[:, 0:512],
                                 wq[:, ch, 128 * dc:128 * dc + 128],
                                 xTs4[:, ch, :],
                                 start=False, stop=(ch == 7))
                    t.mark(("qmm8", cc, dc))

                def emit_qev(cc, dc):
                    v.wait(("qmm8", cc, dc))
                    if cc >= 1 and dc >= 1:
                        v.wait(("smm", cc - 1, dc - 1,
                                chunk_order[cc - 1][-1]))
                    v.tensor_copy(QT[:, qsl(cc, dc), :], mmp[:, 0:512])
                    v.mark(("qev", cc, dc))
                    st2["mm6"] = ("qev", cc, dc)

                def emit_qnrm(cc, dc):
                    if dc == 0 and cc >= 1:
                        v.wait(("negC", cc - 1))
                    v.wait(("qev", cc, dc))
                    v.scalar_tensor_tensor(
                        xb2[:, dc % 2, 0:512], QT[:, qsl(cc, dc), :], 1.0,
                        QT[:, qsl(cc, dc), :], ALU.mult, ALU.mult)
                    v.mark(("sqb", cc, dc))
                    t.wait(("sqb", cc, dc))
                    t.wait(st2["mm7"])
                    t.matmul(mmp[:, 512:1024][0:16, 0:512],
                             qpat_s[:, dc, :], xb2[:, dc % 2, 0:512],
                             start=True, stop=True)
                    t.mark(("qnm2", cc, dc))
                    v.wait(("qnm2", cc, dc))
                    if dc == 0:
                        v.memset(qmax[:], 0.0)
                    v.reduce_max(sm[0:16, C_MQ:C_MQ + 1],
                                 mmp[:, 512:1024][0:16, 0:512], axis=AX.X)
                    v.mark(("qnrr", cc, dc))
                    v.wait(("qnrr", cc, dc))
                    v.tensor_max(qmax[:], qmax[:], sm[0:16, C_MQ:C_MQ + 1])
                    v.mark(("qnmx", cc, dc))
                    st2["mm7"] = ("qnmx", cc, dc)

                def emit_qtrans(cc, tt):
                    ti = 4 * cc + tt
                    if ti >= 2:
                        g.wait(("qxT", ti - 2))
                    if tt < 2:
                        g.wait(("qnm2", cc - 1, 6 + tt))
                    E.gdma(("xb2", ti), xb2[:, ti % 2, :],
                           x_hbm[128 * ti:128 * ti + 128, :])
                    t.wait(("xb2", ti))
                    t.wait(("c", 6))
                    t.wait(st2["mm6"])
                    for ch in range(8):
                        t.transpose(mmp_b[:, 128 * ch:128 * ch + 128],
                                    xb2[:, ti % 2, 128 * ch:128 * ch + 128],
                                    idn[:])
                    t.mark(("qxT", ti))
                    v.wait(("qxT", ti))
                    if tt == 0:
                        v.wait(("qmm8", cc - 1, 7))
                    v.tensor_copy(xTs4[:, :, 128 * tt:128 * tt + 128],
                                  mmp_b[:, 0:1024].rearrange(
                                      "p (c n) -> p c n", n=128))
                    v.mark(("qxTe", ti))
                    st2["mm6"] = ("qxTe", ti)

                def emit_proj_trans(cc, tt):
                    ti = 4 * cc + tt
                    t.wait(("pve", cc, 7))
                    t.wait(st2["mm6"])
                    for ch in range(8):
                        t.transpose(mmp_b[:, 128 * ch:128 * ch + 128],
                                    attO[:, tt, 128 * ch:128 * ch + 128],
                                    idn[:])
                    t.mark(("aT", ti))
                    v.wait(("aT", ti))
                    if ti >= 2:
                        v.wait(("pmm", ti - 2))
                    v.tensor_copy(aTs[:, ti % 2],
                                  mmp_b[:, 0:1024].rearrange(
                                      "p (c n) -> p c n", n=128))
                    v.mark(("aTe", ti))
                    st2["mm6"] = ("aTe", ti)

                def emit_proj_mm_a(cc, tt):
                    ti = 4 * cc + tt
                    if ti >= 2:
                        y.wait(("spill", ti - 2))
                    E.ydma(("xf", ti), xfy[:, ti % 2, :],
                           x_hbm[128 * ti:128 * ti + 128, :])
                    t.wait(("aTe", ti))
                    t.wait(st2["mm6"])
                    t.wait(("wo",))
                    if ti >= 1:
                        t.wait(("y", ti - 1))
                    for ch in range(8):
                        t.matmul(mmp[:, 0:512],
                                 aTs[:, ti % 2, ch, :],
                                 wo[:, ch, 0:512],
                                 start=(ch == 0), stop=(ch == 7))

                def emit_proj_mm(cc, tt):
                    ti = 4 * cc + tt
                    t.wait(st2["mm7"])
                    for ch in range(8):
                        t.matmul(mmp[:, 512:1024],
                                 aTs[:, ti % 2, ch, :],
                                 wo[:, ch, 512:1024],
                                 start=(ch == 0), stop=(ch == 7))
                    t.mark(("pmm", ti))
                    st2["mm7"] = ("y", ti)
                    st2["mm6"] = ("y", ti)
                    v.wait(("pmm", ti))
                    v.wait(("xf", ti))
                    v.scalar_tensor_tensor(
                        xfy[:, ti % 2, :], mmp[:, 0:1024], 1.0,
                        xfy[:, ti % 2, :], ALU.mult, ALU.add)
                    v.mark(("y", ti))
                    pend_ln.append((emit_proj_ln, cc, tt))

                def emit_proj_ln(cc, tt):
                    ti = 4 * cc + tt
                    v.wait(("y", ti))
                    v.bn_stats(bns[:, 0:6], xfy[:, ti % 2, 0:512])
                    v.bn_stats(bns[:, 6:12], xfy[:, ti % 2, 512:1024])
                    v.mark(("ybn", ti))
                    v.wait(("ybn", ti))
                    v.bn_aggr(bns[:, 12:14], bns[:, 0:12])
                    v.mark(("yag", ti))
                    a.wait(("yag", ti))
                    a.activation(smc(C_LV), bns[:, 13:14], AF.Ln,
                                 bias=epsln_s[:])
                    a.mark(("lv", ti))
                    a.activation(smc(C_S1), bns[:, 13:14], AF.Ln)
                    a.mark(("lv0", ti))
                    a.wait(("lv", ti))
                    a.activation(smc(C_RSTD), smc(C_LV), AF.Exp,
                                 scale=-0.5)
                    a.mark(("rstd", ti))
                    a.wait(("lv0", ti))
                    a.activation(smc(C_S1), smc(C_S1), AF.Exp, scale=0.5)
                    a.mark(("e0", ti))
                    v.wait(("rstd", ti))
                    v.tensor_scalar(xfy[:, ti % 2, :], xfy[:, ti % 2, :],
                                    bns[:, 12:13], smc(C_RSTD),
                                    ALU.subtract, ALU.mult)
                    v.mark(("x1a", ti))
                    y.wait(("x1a", ti))
                    E.ydma(("spill", ti),
                           x1_hbm[128 * ti:128 * ti + 128, :],
                           xfy[:, ti % 2, :])
                    v.wait(("e0", ti))
                    v.tensor_mul(smc(C_SS1), smc(C_S1), smc(C_RSTD))
                    v.mark(("s1", ti))
                    v.wait(("s1", ti))
                    v.tensor_scalar(smc(C_T1), smc(C_SS1), 1.0,
                                    EPS_RMS, ALU.mult, ALU.add)
                    v.mark(("t1", ti))
                    v.wait(("t1", ti))
                    v.reciprocal(r1_all[:, ti:ti + 1], smc(C_T1))
                    v.mark(("r1", ti))

                pend_in = []
                pend_end = []
                pend_ln = []

                def drain_in():
                    if pend_in:
                        fn, cc, tt = pend_in.pop(0)
                        fn(cc, tt)

                def drain_end(p, budget=1):
                    n = 0
                    while pend_end and pend_end[0][3] <= p and n < budget:
                        fn, cc, tt, _ = pend_end.pop(0)
                        fn(cc, tt)
                        n += 1

                for c in range(NCH):
                    nj = 4 * c + 4
                    if c == 0:
                        for tt in range(4):
                            ti = tt
                            if ti >= 2:
                                g.wait(("qxT", ti - 2))
                            E.gdma(("xb2", ti), xb2[:, ti % 2, :],
                                   x_hbm[128 * ti:128 * ti + 128, :])
                            t.wait(("xb2", ti))
                            t.wait(("c", 6))
                            t.wait(st2["mm6"])
                            for ch in range(8):
                                t.transpose(
                                    mmp_b[:, 128 * ch:128 * ch + 128],
                                    xb2[:, ti % 2, 128 * ch:128 * ch + 128],
                                    idn[:])
                            t.mark(("qxT", ti))
                            v.wait(("qxT", ti))
                            v.tensor_copy(
                                xTs4[:, :, 128 * tt:128 * tt + 128],
                                mmp_b[:, 0:1024].rearrange(
                                    "p (c n) -> p c n", n=128))
                            v.mark(("qxTe", ti))
                            st2["mm6"] = ("qxTe", ti)
                    def emit_cbound(cc, tt=0):
                        v.wait(("qnmx", cc, 7))
                        v.tensor_copy(smb[0:16, 0:1], qmax[:])
                        v.mark(("qm16b", cc))
                        t.wait(("qm16b", cc))
                        t.wait(st2["mm7"])
                        t.transpose(mmp_b[0:1, 1536:1552], smb[0:16, 0:1],
                                    idn[0:16, 0:16])
                        t.mark(("qmT2", cc))
                        v.wait(("qmT2", cc))
                        v.reduce_max(sm[0:1, C_MQ:C_MQ + 1],
                                     mmp_b[0:1, 1536:1552], axis=AX.X)
                        v.mark(("mq2", cc))
                        v.wait(("mq2", cc))
                        v.wait(("mk2",))
                        v.tensor_mul(sm[0:1, C_C2:C_C2 + 1],
                                     sm[0:1, C_MQ:C_MQ + 1], mk2[:])
                        v.mark(("c2", cc))
                        a.wait(("c2", cc))
                        a.activation(sm[0:1, C_LV:C_LV + 1],
                                     sm[0:1, C_C2:C_C2 + 1], AF.Ln)
                        a.mark(("clv", cc))
                        a.wait(("clv", cc))
                        a.activation(sm[0:1, C_CS:C_CS + 1],
                                     sm[0:1, C_LV:C_LV + 1], AF.Exp,
                                     scale=0.5)
                        a.mark(("cs", cc))
                        t.wait(("cs", cc))
                        t.matmul(mmp[:, 544:545], ones_r[:],
                                 sm[0:1, C_CS:C_CS + 1],
                                 start=True, stop=True)
                        t.mark(("cb", cc))
                        if cc >= 2:
                            v.wait(("exp", cc - 2, 7,
                                    chunk_order[cc - 2][-1]))
                        v.wait(("cb", cc))
                        v.tensor_scalar(negC2[:, cc % 2:cc % 2 + 1],
                                        mmp[:, 544:545], -0.125, None,
                                        ALU.mult)
                        v.mark(("negC", cc))
                        st2["mm7"] = ("negC", cc)

                    if c == 0:
                        for dch in range(16):
                            emit_qtmm(0, dch)
                            if dch % 2 == 1:
                                emit_qev(0, dch // 2)
                                emit_qnrm(0, dch // 2)
                        emit_cbound(0)
                    if c + 1 < NCH:
                        pend_end += [(emit_qtrans, c + 1, tt, 0)
                                     for tt in range(4)]
                        for dc in range(8):
                            mp = max(0, dc - 1)
                            pend_end += [(emit_qtmm, c + 1, 2 * dc, mp),
                                         (emit_qtmm, c + 1, 2 * dc + 1, mp),
                                         (emit_qev, c + 1, dc, mp),
                                         (emit_qnrm, c + 1, dc, mp)]
                        pend_end += [(emit_cbound, c + 1, 0, 6)]

                    norder = list(range(nj))
                    chunk_order[c] = norder
                    pos_of = {jj: k for k, jj in enumerate(norder)}
                    start_pos = {s: min(pos_of[jj]
                                        for jj in range(4 * c + s + 1))
                                 for s in range(4)}
                    stop_pos = {s: max(pos_of[jj]
                                       for jj in range(4 * c + s + 1))
                                for s in range(4)}

                    def prev_key(p, pos):
                        if pos >= 2:
                            return (c, p, norder[pos - 2])
                        if p >= 1:
                            return (c, p - 1, norder[nj - 2 + pos])
                        if c >= 1:
                            po = chunk_order[c - 1]
                            return (c - 1, 7, po[len(po) - 2 + pos])
                        return None

                    if True:
                        def emit_scores(p, pos):
                            j = norder[pos]
                            slot = pos % 2
                            r = j - 4 * c
                            off = 128 * r if r > 0 else 0
                            nvalid = 512 - off
                            scp = scs[slot]
                            t.wait(("qev", c, 7))
                            pk = prev_key(p, pos)
                            if pk is not None:
                                t.wait(("exp",) + pk)
                            t.matmul(scp[:, off:512],
                                     KT[0:64, p, 128 * j:128 * j + 128],
                                     QT[0:64, qsl(c, p), off:512],
                                     start=True, stop=True,
                                     tile_position=(0, 0))
                            t.matmul(scp[:, 512 + off:1024],
                                     KT[64:128, p, 128 * j:128 * j + 128],
                                     QT[64:128, qsl(c, p), off:512],
                                     start=True, stop=True,
                                     tile_position=(64, 0))
                            t.mark(("smm", c, p, j))
                            a.wait(("smm", c, p, j))
                            if pk is not None:
                                a.wait(("pv",) + pk)
                            a.wait(("negC", c))
                            a.activation(
                                bass.AP(p_sb, 1024 * slot + off,
                                        [[2048, 128], [512, 2],
                                         [1, nvalid]]),
                                bass.AP(scp, off,
                                        [[1024, 128], [512, 2],
                                         [1, nvalid]]),
                                AF.Exp, bias=negC2[:, c % 2:c % 2 + 1],
                                scale=0.125)
                            a.mark(("exp", c, p, j))
                            if r >= 0:
                                e_m = g if p == 0 else v
                                e_m.wait(("exp", c, p, j))
                                for h in range(2):
                                    mp = bass.AP(p_sb,
                                                 1024 * slot + 512 * h + off,
                                                 [[2048, 128], [1, 128]])
                                    if p == 0:
                                        e_m.tensor_tensor(mp, mp, tri_s[:],
                                                          ALU.mult)
                                    else:
                                        e_m.tensor_mul(mp, mp, tri_s[:])
                                e_m.mark(("msk", c, p, j))

                        def emit_pv(p, pos):
                            j = norder[pos]
                            slot = pos % 2
                            r = j - 4 * c
                            if r >= 0:
                                t.wait(("msk", c, p, j))
                            else:
                                t.wait(("exp", c, p, j))
                            for h in range(2):
                                if pos == 0:
                                    if p >= 1:
                                        t.wait(("pveh", c, p - 1, h))
                                    elif c >= 1:
                                        t.wait(("pveh", c - 1, 7, h))
                                ten, base, pstr = pv_even[h]
                                for s in range(max(0, r), 4):
                                    t.matmul(
                                        bass.AP(ten, base + 66 * s,
                                                [[pstr, 128], [1, 65]]),
                                        p_sb[:, slot,
                                             512 * h + 128 * s:
                                             512 * h + 128 * s + 128],
                                        VS[:, j, 2 * p + h, 0:65],
                                        start=(pos == 0 and s == 0),
                                        stop=(pos == stop_pos[s]),
                                        skip_group_check=True)
                            t.mark(("pv", c, p, j))

                        def emit_epilogue(p):
                            v.wait(("pv", c, p, norder[-1]))
                            if c >= 1:
                                v.wait(("aT", 4 * (c - 1) + 3))
                            for h in range(2):
                                ten, base, pstr = pv_even[h]
                                v.reciprocal(recs[:, 4 * h:4 * h + 4],
                                             bass.AP(ten, base + 64,
                                                     [[pstr, 128], [66, 4]]))
                                v.mark(("rc", c, p, h))
                                v.wait(("rc", c, p, h))
                                for s in range(4):
                                    v.tensor_scalar(
                                        attO[:, s, 128 * p + 64 * h:
                                             128 * p + 64 * h + 64],
                                        bass.AP(ten, base + 66 * s,
                                                [[pstr, 128], [1, 64]]),
                                        recs[:, 4 * h + s:4 * h + s + 1],
                                        None, ALU.mult)
                                v.mark(("pveh", c, p, h))
                            v.mark(("pve", c, p), inc=0)
                            while (pend_end and pend_end[0][0] is emit_qev
                                   and pend_end[0][3] <= p):
                                fn, cc2, tt2, _ = pend_end.pop(0)
                                fn(cc2, tt2)
                            while pend_in:
                                drain_in()

                        seq = [(p, k) for p in range(8) for k in range(nj)]
                        for gx, (p, k) in enumerate(seq):
                            if k == 3 and pend_ln:
                                fn, cc2, tt2 = pend_ln.pop(0)
                                fn(cc2, tt2)
                            if p == 0 and k >= 2:
                                drain_in()
                                if nj <= 8:
                                    drain_in()
                            elif p >= 1 and k >= 4 and (k % 2 == 0
                                                        or nj <= 16):
                                drain_end(p - 1,
                                          budget=(2 if nj <= 12 else 1))
                            emit_scores(p, k)
                            if gx >= 2:
                                p2, k2 = seq[gx - 2]
                                emit_pv(p2, k2)
                                if k2 == nj - 1:
                                    emit_epilogue(p2)
                        for gx in (len(seq) - 2, len(seq) - 1):
                            p2, k2 = seq[gx]
                            emit_pv(p2, k2)
                            if k2 == nj - 1:
                                emit_epilogue(p2)
                    while pend_end:
                        drain_end(99)
                    pend_in = [(emit_proj_trans, c, 0),
                               (emit_proj_trans, c, 1),
                               (emit_proj_mm_a, c, 0),
                               (emit_proj_mm, c, 0),
                               (emit_proj_trans, c, 2),
                               (emit_proj_mm_a, c, 1),
                               (emit_proj_mm, c, 1),
                               (emit_proj_trans, c, 3)]
                    pend_end = [(emit_proj_mm_a, c, 2, 0),
                                (emit_proj_mm, c, 2, 0),
                                (emit_proj_mm_a, c, 3, 1),
                                (emit_proj_mm, c, 3, 1)]

                while pend_in:
                    drain_in()
                while pend_end or pend_ln:
                    while pend_ln:
                        fn, cc2, tt2 = pend_ln.pop(0)
                        fn(cc2, tt2)
                    if pend_end:
                        drain_end(99)

                E.y.wait(("spill", NT - 1))
                E.y.wait(("spill", NT - 2))

            def trx_slice(mb):
                return mb[0:1, 512:528]

            _emit(nc, "phase2", phase2, P)

        nc.all_engine_barrier()
        mctx.close()

        with ExitStack() as ctx:
            def sb(name, shape, dtype):
                return ctx.enter_context(nc.sbuf_tensor(name, shape, dtype))

            wupT = sb("wupT", [128, 8, FF], BF16)
            wdnT = sb("wdnT", [128, 32, D], BF16)
            wr3 = sb("wr3", [128, 3, D], F32)
            m1b = sb("m1b", [128, 2, D], BF16)
            m2b = sb("m2b", [128, 1, D], BF16)
            absw = sb("absw", [128, D], F32)
            wtb = sb("wtb", [128, 2, D], BF16)
            x1b = sb("x1b", [128, 2, D], BF16)
            x1f = sb("x1f", [128, 2, D], F32)
            x1Ts = sb("x1Ts", [128, 2, 8, 128], BF16)
            h_sb = sb("h_sb", [128, 2, FF], BF16)
            hTs = sb("hTs", [128, 32, 128], BF16)
            outs = sb("outs", [128, 2, D], F32)
            sqd3 = sb("sqd3", [128, 2048], BF16)

            upq = [ctx.enter_context(
                nc.psum_tensor(f"upq{i}", [128, 512], F32))
                for i in range(2)]
            tra = ctx.enter_context(nc.psum_tensor("tra", [128, 512], F32))
            trb = ctx.enter_context(nc.psum_tensor("trb", [128, 512], F32))
            dwn2 = [ctx.enter_context(
                nc.psum_tensor(f"dwn{i}", [128, 1024], F32))
                for i in range(2)]
            tra_b = tra.bitcast(BF16)
            trb_b = trb.bitcast(BF16)

            def phase3(E):
                t, v, a, g, y = E.t, E.v, E.a, E.g, E.y
                a.activation(smc(C_SD), ones_c[:], AF.Gelu)
                a.mark(("apin3",))

                state = {"tra": None, "upq": [None, None]}

                def tern_chunk(r):
                    if r >= 3:
                        y.wait(("tern", r - 3))
                    if r < 32:
                        src = wup_hbm[128 * r:128 * r + 128, :]
                        thr = thr4[:, 0:1]
                    else:
                        rr = r - 32
                        qq, dc = rr // 8, rr % 8
                        src = wdn_hbm[128 * dc:128 * dc + 128,
                                      1024 * qq:1024 * qq + 1024]
                        thr = thr4[:, 1:2]
                    E.ydma(("w3", r), wr3[:, r % 3, :], src)
                    a.wait(("w3", r))
                    a.wait(("apin3",))
                    if r >= 2:
                        a.wait(("tern", r - 2))
                    a.activation(m1b[:, r % 2, :], wr3[:, r % 3, :], AF.Sign)
                    a.mark(("sgn", r))
                    if r >= 1:
                        a.wait(("mskt", r - 1))
                    a.activation(absw[:], wr3[:, r % 3, :], AF.Abs)
                    a.mark(("absw", r))
                    v.wait(("absw", r))
                    if r >= 2:
                        v.wait(("tern", r - 2))
                    v.tensor_scalar(m2b[:, 0, :], absw[:],
                                    thr, None, ALU.is_gt)
                    v.mark(("mskt", r))
                    v.wait(("mskt", r))
                    v.wait(("sgn", r))
                    if r >= 2:
                        v.wait(("wTr", r - 2))
                    v.tensor_mul(wtb[:, r % 2, :], m1b[:, r % 2, :],
                                 m2b[:, 0, :])
                    v.mark(("tern", r))
                    t.wait(("tern", r))
                    t.wait(state["tra"])
                    for ch in range(8):
                        t.transpose(tra_b[:, 128 * ch:128 * ch + 128],
                                    wtb[:, r % 2, 128 * ch:128 * ch + 128],
                                    idn[:])
                    t.mark(("wTr", r))
                    v.wait(("wTr", r))
                    if r < 32:
                        dst = wupT[:, :, 128 * r:128 * r + 128]
                    else:
                        rr = r - 32
                        qq, dc = rr // 8, rr % 8
                        dst = wdnT[:, 8 * qq:8 * qq + 8,
                                   128 * dc:128 * dc + 128]
                    v.tensor_copy(dst, tra_b[:, 0:1024].rearrange(
                        "p (c n) -> p c n", n=128))
                    v.mark(("wTe", r))
                    state["tra"] = ("wTe", r)

                upq_hist = []

                def issue_x1b(i):
                    g.wait(("spill", i))
                    if i >= 2:
                        g.wait(("x1T", i - 2))
                    E.gdma(("x1b", i), x1b[:, i % 2, :],
                           x1_hbm[128 * i:128 * i + 128, :])

                def up_trans(i):
                    t.wait(("x1b", i))
                    t.wait(state["tra"])
                    for ch in range(8):
                        t.transpose(tra_b[:, 128 * ch:128 * ch + 128],
                                    x1b[:, i % 2, 128 * ch:128 * ch + 128],
                                    idn[:])
                    t.mark(("x1T", i))
                    v.wait(("x1T", i))
                    if i >= 2:
                        v.wait(("up", i - 2, 7))
                    v.tensor_copy(x1Ts[:, i % 2], tra_b[:, 0:1024].rearrange(
                        "p (c n) -> p c n", n=128))
                    v.mark(("x1Te", i))
                    state["tra"] = ("x1Te", i)
                    if i >= 2:
                        y.wait(("o", i - 2))
                    y.wait(("spill", i))
                    E.ydma(("x1f", i), x1f[:, i % 2, :],
                           x1_hbm[128 * i:128 * i + 128, :])

                def up_part_q(i, q):
                    if q == 0:
                        t.wait(("x1Te", i))
                    t.wait(("wTe", 4 * q + 3))
                    bank = len(upq_hist) % 2
                    if len(upq_hist) >= 2:
                        t.wait(("gel",) + upq_hist[-2])
                    upq_hist.append((i, q))
                    for ch in range(8):
                        t.matmul(upq[bank][:, 0:512],
                                 x1Ts[:, i % 2, ch, :],
                                 wupT[:, ch, 512 * q:512 * q + 512],
                                 start=(ch == 0), stop=(ch == 7))
                    t.mark(("up", i, q))
                    a.wait(("up", i, q))
                    if i >= 2:
                        a.wait(("hT", i - 2, q // 2))
                    a.activation(h_sb[:, i % 2, 512 * q:512 * q + 512],
                                 upq[bank][:, 0:512], AF.Gelu,
                                 scale=r1_all[:, i:i + 1])
                    a.mark(("gel", i, q))

                def up_part(i):
                    for q in range(8):
                        up_part_q(i, q)

                def down_part(i):
                    def hround(k):
                        t.wait(("gel", i, 2 * k + 1))
                        bb = (trb_b, tra_b)[k % 2]
                        if k % 2 == 0:
                            if i >= 1 or k >= 2:
                                pk = (i, k - 2) if k >= 2 else (i - 1, 2)
                                t.wait(("hTe",) + pk)
                        else:
                            t.wait(state["tra"])
                        for u in range(8):
                            t.transpose(bb[:, 128 * u:128 * u + 128],
                                        h_sb[:, i % 2,
                                             1024 * k + 128 * u:
                                             1024 * k + 128 * u + 128],
                                        idn[:])
                        t.mark(("hT", i, k))
                        ev = a if k % 2 == 0 else v
                        ev.wait(("hT", i, k))
                        ev_src = bb[:, 0:1024].rearrange(
                            "p (c n) -> p c n", n=128)
                        if k % 2 == 0:
                            ev.activation(hTs[:, 8 * k:8 * k + 8, :],
                                          ev_src, AF.Copy)
                        else:
                            ev.tensor_copy(hTs[:, 8 * k:8 * k + 8, :],
                                           ev_src)
                        ev.mark(("hTe", i, k))
                        if k % 2 == 1:
                            state["tra"] = ("hTe", i, k)

                    def dround(k):
                        if k == 0 and i >= 2:
                            t.wait(("y2", i - 2))
                        t.wait(("hTe", i, k))
                        t.wait(("wTe", 32 + 8 * k + 7))
                        for fc in range(8 * k, 8 * k + 8):
                            for nb in range(2):
                                t.matmul(
                                    dwn2[i % 2][:, 512 * nb:512 * nb + 512],
                                    hTs[:, fc, :],
                                    wdnT[:, fc, 512 * nb:512 * nb + 512],
                                    start=(fc == 0), stop=(fc == 31))

                    hround(0)
                    hround(1)
                    dround(0)
                    hround(2)
                    dround(1)
                    hround(3)
                    dround(2)
                    dround(3)
                    t.mark(("dn", i))
                    a.wait(("gel", i, 7))
                    if i >= 1:
                        a.wait(("y2ss", i - 1))
                    a.activation(sqd3[:], h_sb[:, i % 2, 0:2048], AF.Square,
                                 accum_out=smc(C_SS2))
                    a.mark(("hssa", i))
                    a.wait(("hssa", i))
                    a.activation(sqd3[:], h_sb[:, i % 2, 2048:4096],
                                 AF.Square, accum_out=sm[:, 22:23])
                    a.mark(("hss", i))
                    v.wait(("hss", i))
                    v.tensor_tensor(smc(C_SS2), smc(C_SS2), sm[:, 22:23],
                                    ALU.add)
                    v.mark(("hsum", i))
                    g.wait(("hsum", i))
                    g.tensor_tensor(smc(C_S2), smc(C_SS2), half_s[:],
                                    ALU.pow)
                    g.mark(("s2", i))
                    v.wait(("s2", i))
                    v.tensor_scalar(smc(C_T1), smc(C_S2), 1.0 / 64.0,
                                    EPS_RMS, ALU.mult, ALU.add)
                    v.mark(("t2", i))
                    v.wait(("t2", i))
                    v.reciprocal(smc(C_R2), smc(C_T1))
                    v.mark(("r2", i))
                    v.wait(("dn", i))
                    v.wait(("x1f", i))
                    v.wait(("r2", i))
                    v.scalar_tensor_tensor(
                        x1f[:, i % 2, :], dwn2[i % 2][:, 0:1024], smc(C_R2),
                        x1f[:, i % 2, :], ALU.mult, ALU.add,
                        accum_out=smc(C_YSUM))
                    v.mark(("y2", i))
                    a.wait(("y2", i))
                    a.activation(sqd3[:, 0:D], x1f[:, i % 2, :], AF.Square,
                                 accum_out=smc(C_YSS))
                    a.mark(("y2ss", i))
                    v.wait(("y2", i))
                    v.tensor_scalar(smc(C_M), smc(C_YSUM), 1.0 / D, None,
                                    ALU.mult)
                    v.mark(("m2", i))
                    a.wait(("m2", i))
                    a.activation(smc(C_MSQ), smc(C_M), AF.Square)
                    a.mark(("msq2", i))
                    v.wait(("y2ss", i))
                    v.wait(("msq2", i))
                    v.tensor_scalar(smc(C_VAR), smc(C_YSS), 1.0 / D,
                                    smc(C_MSQ), ALU.mult, ALU.subtract)
                    v.mark(("var2", i))
                    v.wait(("var2", i))
                    v.tensor_scalar(smc(C_VE), smc(C_VAR), EPS_LN, None,
                                    ALU.add)
                    v.mark(("ve2", i))
                    g.wait(("ve2", i))
                    g.tensor_tensor(smc(C_SD), smc(C_VE), half_s[:],
                                    ALU.pow)
                    g.mark(("sd2", i))
                    v.wait(("sd2", i))
                    v.reciprocal(smc(C_RSTD), smc(C_SD))
                    v.mark(("rstd2", i))
                    v.wait(("rstd2", i))
                    if i >= 2:
                        v.wait(("oD", i - 2))
                    v.tensor_scalar(outs[:, i % 2, :], x1f[:, i % 2, :],
                                    smc(C_M), smc(C_RSTD), ALU.subtract,
                                    ALU.mult)
                    v.mark(("o", i))
                    y.wait(("o", i))
                    E.ydma(("oD", i), out_hbm[128 * i:128 * i + 128, :],
                           outs[:, i % 2, :])

                for r in range(32):
                    tern_chunk(r)
                issue_x1b(0)
                up_trans(0)
                up_part(0)
                issue_x1b(1)
                up_trans(1)
                up_part(1)
                issue_x1b(2)
                for r in range(32, 64):
                    tern_chunk(r)
                for i in range(NT):
                    if i + 3 < NT:
                        issue_x1b(i + 3)
                    if i + 2 < NT:
                        up_trans(i + 2)
                    down_part(i)
                    if i + 2 < NT:
                        up_part(i + 2)
                y.wait(("oD", NT - 2))
                y.wait(("oD", NT - 1))

            _emit(nc, "phase3", phase3, P)

    return nc


_NC_CACHE = {}


def kernel(L0_attn_x, L0_attn_w_qkv, L0_attn_w_out, L0_ffn_w_up,
           L0_ffn_w_down):
    x = np.ascontiguousarray(np.asarray(L0_attn_x, dtype=np.float32))
    wqkv = np.ascontiguousarray(np.asarray(L0_attn_w_qkv, dtype=np.float32))
    wout = np.ascontiguousarray(np.asarray(L0_attn_w_out, dtype=np.float32))
    wup = np.ascontiguousarray(np.asarray(L0_ffn_w_up, dtype=np.float32))
    wdn = np.ascontiguousarray(np.asarray(L0_ffn_w_down, dtype=np.float32))
    B = x.shape[0]
    if "nc" not in _NC_CACHE:
        _NC_CACHE["nc"] = build_program()
    nc = _NC_CACHE["nc"]
    in_maps = [{"x": np.ascontiguousarray(x[b]), "w_qkv": wqkv,
                "w_out": wout, "w_up": wup, "w_down": wdn}
               for b in range(B)]
    res = run_bass_kernel_spmd(nc, in_maps, core_ids=list(range(B)))
    out = np.stack([np.asarray(res.results[b]["out"]) for b in range(B)],
                   axis=0)
    return out.astype(np.float32)


# version 33
# speedup vs baseline: 1.0089x; 1.0089x over previous
import numpy as np
import ml_dtypes
from contextlib import ExitStack

import concourse.bass as bass
import concourse.mybir as mybir
from concourse.bass_utils import run_bass_kernel_spmd

F32 = mybir.dt.float32
BF16 = mybir.dt.bfloat16
AF = mybir.ActivationFunctionType
ALU = mybir.AluOpType
AX = mybir.AxisListType

S = 4096
D = 1024
H = 16
FF = 4096
NT = S // 128
NCH = S // 512
EPS_LN = 1e-5
EPS_RMS = 1e-8
ENGS = ("t", "v", "a", "g", "y")


class _Eng:

    def __init__(self, prog, name):
        object.__setattr__(self, "prog", prog)
        object.__setattr__(self, "name", name)
        object.__setattr__(self, "real", None)
        object.__setattr__(self, "last", None)

    def __getattr__(self, op):
        def fn(*args, **kw):
            if self.real is not None:
                object.__setattr__(self, "last",
                                   getattr(self.real, op)(*args, **kw))
            return self.last
        return fn

    def mark(self, key, inc=1):
        prog = self.prog
        if prog.planning:
            prog.ctr[self.name] += inc
            assert key not in prog.marks, f"dup mark {key}"
            prog.marks[key] = (self.name, prog.ctr[self.name])
        elif self.real is not None and inc:
            self.last.then_inc(prog.sems[self.name], inc)

    def wait(self, key):
        prog = self.prog
        if prog.planning or key is None or self.real is None:
            return
        src, val = prog.marks[key]
        seen = prog.waited.get((self.name, src), 0)
        if val <= seen:
            return
        prog.waited[(self.name, src)] = val
        self.real.wait_ge(prog.sems[src], val)


DMA_STREAMS = {"c": 1, "wkv": 1, "wq": 1, "wo": 1, "xb": 2, "aw": 2,
               "xb2": 2, "xf": 2, "spill": 2, "w3": 3, "x1b": 2,
               "x1f": 2, "oD": 2}


class _Prog:
    def __init__(self):
        self.marks = {}
        self.ctr = {n: 0 for n in ENGS}
        self.planning = True
        self.sems = {}
        self.waited = {}
        self.snum = {}
        for n in ENGS:
            setattr(self, n, _Eng(self, n))

    def _dma(self, eng, stream, key, out, in_):
        ring = DMA_STREAMS[stream]
        if self.planning:
            o = self.snum.get(stream, 0)
            self.snum[stream] = o + 1
            sname = f"q_{stream}{o % ring}"
            assert key not in self.marks, key
            self.marks[key] = (sname, 16 * (o // ring + 1))
        if eng.real is not None:
            ins = eng.real.dma_start(out, in_)
            object.__setattr__(eng, "last", ins)
            sname, _ = self.marks[key]
            ins.then_inc(self.sems[sname], 16)

    def gdma(self, key, out, in_, stream=None):
        self._dma(self.g, stream or key[0], key, out, in_)

    def ydma(self, key, out, in_, stream=None):
        self._dma(self.y, stream or key[0], key, out, in_)


def _emit(nc, block_name, sched, P):
    P.planning = True
    sched(P)
    P.planning = False
    with nc.Block(block_name) as block:
        def run(live, real):
            for n in ENGS:
                object.__setattr__(getattr(P, n), "real", None)
            object.__setattr__(getattr(P, live), "real", real)
            P.waited = {}
            sched(P)
            object.__setattr__(getattr(P, live), "real", None)
        block.tensor(lambda e: run("t", e))
        block.vector(lambda e: run("v", e))
        block.scalar(lambda e: run("a", e))
        block.gpsimd(lambda e: run("g", e))
        block.sync(lambda e: run("y", e))


def build_program():
    nc = bass.Bass("TRN2", target_bir_lowering=False)

    x_hbm = nc.dram_tensor("x", [S, D], F32, kind="ExternalInput")
    wqkv_hbm = nc.dram_tensor("w_qkv", [D, 3 * D], F32, kind="ExternalInput")
    wout_hbm = nc.dram_tensor("w_out", [D, D], F32, kind="ExternalInput")
    wup_hbm = nc.dram_tensor("w_up", [FF, D], F32, kind="ExternalInput")
    wdn_hbm = nc.dram_tensor("w_down", [D, FF], F32, kind="ExternalInput")
    out_hbm = nc.dram_tensor("out", [S, D], F32, kind="ExternalOutput")
    x1_hbm = nc.dram_tensor("x1spill", [S, D], F32)

    idn_h = nc.inline_tensor(np.eye(128, dtype=ml_dtypes.bfloat16), "idn")
    tri = np.triu(np.ones((128, 128), np.float32)).astype(ml_dtypes.bfloat16)
    tri_h = nc.inline_tensor(tri, "tri2")
    ones_r_h = nc.inline_tensor(np.ones((1, 128), np.float32), "ones_r")
    ones_c_h = nc.inline_tensor(np.ones((128, 1), np.float32), "ones_c")
    half_h = nc.inline_tensor(np.full((128, 1), 0.5, np.float32), "half")
    qpat_np = np.zeros((128, 8, 16), np.float32)
    for ch in range(8):
        qpat_np[0:64, ch, 2 * ch] = 1.0
        qpat_np[64:128, ch, 2 * ch + 1] = 1.0
    qpat_h = nc.inline_tensor(qpat_np.astype(ml_dtypes.bfloat16), "qpat")
    epsln_h = nc.inline_tensor(np.full((128, 1), EPS_LN, np.float32), "epsln")

    P = _Prog()

    with ExitStack() as octx:
        def osb(name, shape, dtype):
            return octx.enter_context(nc.sbuf_tensor(name, shape, dtype))

        for n in ENGS:
            P.sems[n] = octx.enter_context(nc.semaphore("sem_" + n))
        for sname, ring in DMA_STREAMS.items():
            for i in range(ring):
                P.sems[f"q_{sname}{i}"] = octx.enter_context(
                    nc.semaphore(f"q_{sname}{i}"))

        r1_all = osb("r1_all", [128, NT], F32)
        idn = osb("idn_s", [128, 128], BF16)
        ones_r = osb("ones_r_s", [1, 128], F32)
        ones_c = osb("ones_c_s", [128, 1], F32)
        half_s = osb("half_s", [128, 1], F32)
        epsln_s = osb("epsln_s", [128, 1], F32)
        qpat_s = osb("qpat_s", [128, 8, 16], BF16)
        thr4 = osb("thr4", [128, 4], F32)
        sm = osb("sm", [128, 23], F32)
        bns = osb("bns", [128, 14], F32)
        mctx = octx.enter_context(ExitStack())

        def msb(name, shape, dtype):
            return mctx.enter_context(nc.sbuf_tensor(name, shape, dtype))

        KT = msb("KT", [128, 8, S], BF16)
        VS = msb("VS", [128, NT, H, 65], BF16)
        tri_s = msb("tri_s", [128, 128], BF16)
        mk2 = msb("mk2", [1, 1], F32)
        smb = msb("smb", [128, 16], BF16)

        (C_AU, C_AD, C_TOT, C_TOT2, C_MQ, C_C2, C_CS, C_M, C_MSQ, C_VAR,
         C_LV, C_RSTD, C_SS1, C_S1, C_T1, C_YSUM, C_YSS, C_SS2, C_S2,
         C_R2, C_VE, C_SD) = range(22)

        def smc(i, n=1):
            return sm[:, i:i + n]

        with ExitStack() as ctx:
            def sb(name, shape, dtype):
                return ctx.enter_context(nc.sbuf_tensor(name, shape, dtype))

            wkv = sb("wkv", [128, 8, 2 * D], BF16)
            sm16 = sb("sm16", [128, 16], F32)
            xb1 = sb("xb1", [128, 2, D], BF16)
            xTs = sb("xTs", [128, 2, 8, 128], BF16)
            kbuf = sb("kbuf", [128, 2, D], BF16)
            sqK = sb("sqK", [128, D], BF16)
            sqA = sb("sqA", [128, D], BF16)
            wr1 = sb("wr1", [128, 2, D], F32)
            aasU = sb("aasU", [128, 32], F32)
            aasD = sb("aasD", [128, 32], F32)
            kmax = sb("kmax", [128, H], F32)

            kvp = ctx.enter_context(nc.psum_tensor("kvp", [128, 2048], F32))
            trp = ctx.enter_context(nc.psum_tensor("trp", [128, 512], F32))
            trq = ctx.enter_context(nc.psum_tensor("trq", [128, 512], F32))
            tru = ctx.enter_context(nc.psum_tensor("tru", [128, 512], F32))
            trv = ctx.enter_context(nc.psum_tensor("trv", [128, 512], F32))
            trp_b = trp.bitcast(BF16)
            trq_b = trq.bitcast(BF16)
            tru_b = tru.bitcast(BF16)
            trv_b = trv.bitcast(BF16)

            def phase1(E):
                t, v, a, g, y = E.t, E.v, E.a, E.g, E.y
                E.ydma(("c", 0), idn[:], idn_h[:])
                E.ydma(("c", 1), tri_s[:], tri_h[:])
                E.ydma(("c", 2), ones_r[:], ones_r_h[:])
                E.ydma(("c", 3), ones_c[:], ones_c_h[:])
                E.ydma(("c", 4), half_s[:], half_h[:])
                E.ydma(("c", 5), epsln_s[:], epsln_h[:])
                E.ydma(("c", 6), qpat_s[:], qpat_h[:])
                E.gdma(("wkv",), wkv[:],
                       wqkv_hbm[:, D:3 * D].rearrange("(c p) n -> p c n",
                                                      p=128))
                v.memset(kmax[:], 0.0)
                v.mark(("km0",))
                v.memset(VS[:], 1.0)
                v.mark(("vs1",))
                a.wait(("c", 6))
                a.activation(smc(C_SD), ones_c[:], AF.Exp)
                a.mark(("apin",))

                def k_trans(i):
                    t.wait(("ke", i))
                    if i >= 2:
                        t.wait(("kte", i - 2))
                    ktb = (tru_b, trv_b)[i % 2]
                    for c in range(8):
                        t.transpose(ktb[:, 128 * c:128 * c + 128],
                                    kbuf[:, i % 2, 128 * c:128 * c + 128],
                                    idn[:])
                    t.mark(("ktr", i))
                    v.wait(("ktr", i))
                    v.tensor_copy(KT[:, :, 128 * i:128 * i + 128],
                                  ktb[:, 0:1024].rearrange(
                                      "p (c n) -> p c n", n=128))
                    v.mark(("kte", i))

                for i in range(NT):
                    if i >= 2:
                        g.wait(("xT", i - 2))
                    E.gdma(("xb", i), xb1[:, i % 2, :],
                           x_hbm[128 * i:128 * i + 128, :])
                    t.wait(("xb", i))
                    t.wait(("c", 6))
                    if i >= 2:
                        t.wait(("xTe", i - 2))
                    xtb = (trp_b, trq_b)[i % 2]
                    for c in range(8):
                        t.transpose(xtb[:, 128 * c:128 * c + 128],
                                    xb1[:, i % 2, 128 * c:128 * c + 128],
                                    idn[:])
                    t.mark(("xT", i))
                    v.wait(("xT", i))
                    if i >= 2:
                        v.wait(("kv", i - 2))
                    v.tensor_copy(xTs[:, i % 2],
                                  xtb[:, 0:1024].rearrange(
                                      "p (c n) -> p c n", n=128))
                    v.mark(("xTe", i))
                    if i >= 1:
                        k_trans(i - 1)
                    t.wait(("xTe", i))
                    t.wait(("wkv",))
                    for nb in range(4):
                        if nb == 0 and i >= 1:
                            t.wait(("ke", i - 1))
                        if nb == 2 and i >= 1:
                            t.wait(("ve", i - 1))
                        for c in range(8):
                            t.matmul(kvp[:, 512 * nb:512 * nb + 512],
                                     xTs[:, i % 2, c, :],
                                     wkv[:, c, 512 * nb:512 * nb + 512],
                                     start=(c == 0), stop=(c == 7))
                        if nb == 1:
                            t.mark(("kvK", i))
                    t.mark(("kv", i))
                    a.wait(("kvK", i))
                    if i >= 2:
                        a.wait(("ktr", i - 2))
                        a.wait(("knr", i - 2))
                    a.activation(kbuf[:, i % 2, :], kvp[:, 0:D], AF.Copy)
                    a.mark(("ke", i))
                    a.wait(("kv", i))
                    a.wait(("vs1",))
                    a.activation(VS[:, i, :, 0:64],
                                 kvp[:, D:2 * D].rearrange(
                                     "p (h e) -> p h e", e=64), AF.Copy)
                    a.mark(("ve", i))
                    for ii in ([i - 1] if i >= 1 else []) + \
                            ([i] if i == NT - 1 else []):
                        v.wait(("ve", ii))
                        v.scalar_tensor_tensor(sqK[:], kbuf[:, ii % 2, :],
                                               1.0, kbuf[:, ii % 2, :],
                                               ALU.mult, ALU.mult)
                        v.mark(("ksq", ii))
                        v.wait(("ksq", ii))
                        v.reduce_sum(sm16[:], sqK[:].rearrange(
                            "p (h e) -> p h e", e=64), axis=AX.X)
                        v.mark(("knr", ii))
                        v.wait(("knr", ii))
                        v.wait(("km0",))
                        v.tensor_max(kmax[:], kmax[:], sm16[:])
                        v.mark(("kmx", ii))
                    for r in (2 * i, 2 * i + 1):
                        if r >= 2:
                            y.wait(("aact", r - 2))
                        if r < 32:
                            src_ap = wup_hbm[128 * r:128 * r + 128, :]
                        else:
                            rr = r - 32
                            src_ap = wdn_hbm[
                                128 * (rr // 4):128 * (rr // 4) + 128,
                                1024 * (rr % 4):1024 * (rr % 4) + 1024]
                        E.ydma(("aw", r), wr1[:, r % 2, :], src_ap)
                        a.wait(("aw", r))
                        if r >= 1:
                            a.wait(("aact", r - 1))
                        acc = (aasU[:, r:r + 1] if r < 32
                               else aasD[:, r - 32:r - 31])
                        a.activation(sqA[:], wr1[:, r % 2, :], AF.Abs,
                                     accum_out=acc)
                        a.mark(("aact", r))

                k_trans(NT - 1)

                v.wait(("aact", 31))
                v.reduce_sum(smc(C_AU), aasU[:], axis=AX.X)
                v.mark(("redU",))
                v.wait(("aact", 63))
                v.reduce_sum(smc(C_AD), aasD[:], axis=AX.X)
                v.mark(("redD",))
                t.wait(("redU",))
                t.wait(("redD",))
                t.wait(("c", 6))
                t.wait(("ve", NT - 1))
                t.matmul(kvp[0:1, 0:1], smc(C_AU), ones_c[:],
                         start=True, stop=True)
                t.matmul(kvp[0:1, 1:2], smc(C_AD), ones_c[:],
                         start=True, stop=True)
                t.mark(("atot",))
                v.wait(("atot",))
                v.tensor_scalar(sm[0:1, C_TOT:C_TOT + 2], kvp[0:1, 0:2],
                                0.5 / float(FF * D), None, ALU.mult)
                v.mark(("tot01",))
                t.wait(("tot01",))
                t.matmul(kvp[:, 2:4], ones_r[:], sm[0:1, C_TOT:C_TOT + 2],
                         start=True, stop=True)
                t.mark(("thrb",))
                v.wait(("thrb",))
                v.tensor_copy(thr4[:, 0:2], kvp[:, 2:4])
                v.mark(("thr",))
                v.wait(("thr",))
                v.tensor_scalar(thr4[:, 2:4], thr4[:, 0:2], -1.0, None,
                                ALU.mult)
                v.mark(("thrn",))

                v.wait(("kmx", NT - 1))
                v.tensor_copy(smb[:], kmax[:])
                v.mark(("kmb",))
                t.wait(("kmb",))
                t.wait(("xTe", NT - 1))
                t.transpose(trp_b[0:16, 0:128], smb[:], idn[:])
                t.mark(("kmT",))
                v.wait(("kmT",))
                v.reduce_max(sm[0:16, C_MQ:C_MQ + 1], trp_b[0:16, 0:128],
                             axis=AX.X)
                v.mark(("km16",))
                v.wait(("km16",))
                v.tensor_copy(smb[0:16, 0:1], sm[0:16, C_MQ:C_MQ + 1])
                v.mark(("km16b",))
                t.wait(("km16b",))
                t.wait(("kte", NT - 1))
                t.transpose(trq_b[0:1, 0:16], smb[0:16, 0:1], idn[0:16, 0:16])
                t.mark(("kmT2",))
                v.wait(("kmT2",))
                v.reduce_max(mk2[:], trq_b[0:1, 0:16], axis=AX.X)
                v.mark(("mk2",))

            _emit(nc, "phase1", phase1, P)

        nc.all_engine_barrier()

        with ExitStack() as ctx:
            def sb(name, shape, dtype):
                return ctx.enter_context(nc.sbuf_tensor(name, shape, dtype))

            wq = sb("wq", [128, 8, D], BF16)
            wo = sb("wo", [128, 8, D], BF16)
            QT = sb("QT", [128, 9, 512], BF16)
            xb2 = sb("xb2", [128, 2, D], BF16)
            xfy = sb("xfy", [128, 2, D], F32)
            xTs4 = sb("xTs4", [128, 8, 512], BF16)
            aTs = sb("aTs", [128, 2, 8, 128], BF16)
            p_sb = sb("p_sb", [128, 2, 1024], BF16)
            attO = sb("attO", [128, 4, D], BF16)
            recs = sb("recs", [128, 8], F32)
            negC2 = sb("negC2", [128, 2], F32)
            qmax = sb("qmax", [16, 1], F32)

            sc0 = ctx.enter_context(nc.psum_tensor("sc0", [128, 1024], F32))
            sc1 = ctx.enter_context(nc.psum_tensor("sc1", [128, 1024], F32))
            pvA = ctx.enter_context(nc.psum_tensor("pvA", [128, 4, 66], F32))
            pvB = ctx.enter_context(nc.psum_tensor("pvB", [128, 4, 66], F32))
            mmp = ctx.enter_context(nc.psum_tensor("mmp", [128, 1024], F32))
            mmp_b = mmp.bitcast(BF16)
            sc0_b = sc0.bitcast(BF16)
            scs = (sc0, sc1)
            pv_even = ((pvA, 0, 264), (pvB, 0, 264))
            pv_odd = pv_even

            def phase2(E):
                t, v, a, g, y = E.t, E.v, E.a, E.g, E.y
                E.gdma(("wq",), wq[:],
                       wqkv_hbm[:, 0:D].rearrange("(c p) n -> p c n", p=128))
                E.gdma(("wo",), wo[:],
                       wout_hbm.rearrange("(c p) n -> p c n", p=128))

                st2 = {"mm6": None, "mm7": None}
                chunk_order = {}

                def qsl(cc, i):
                    return (8 * cc + i) % 9

                def emit_qtmm(cc, dch):
                    dc, half = dch // 2, dch % 2
                    if half == 0:
                        t.wait(("qxTe", 4 * cc + 3))
                        t.wait(("wq",))
                        t.wait(st2["mm6"])
                        for ch in range(4):
                            t.matmul(mmp[:, 0:512],
                                     wq[:, ch, 128 * dc:128 * dc + 128],
                                     xTs4[:, ch, :],
                                     start=(ch == 0), stop=False)
                        return
                    for ch in range(4, 8):
                        t.matmul(mmp[:, 0:512],
                                 wq[:, ch, 128 * dc:128 * dc + 128],
                                 xTs4[:, ch, :],
                                 start=False, stop=(ch == 7))
                    t.mark(("qmm8", cc, dc))

                def emit_qev(cc, dc):
                    v.wait(("qmm8", cc, dc))
                    if cc >= 1 and dc >= 1:
                        v.wait(("smm", cc - 1, dc - 1,
                                chunk_order[cc - 1][-1]))
                    v.tensor_copy(QT[:, qsl(cc, dc), :], mmp[:, 0:512])
                    v.mark(("qev", cc, dc))
                    st2["mm6"] = ("qev", cc, dc)

                def emit_qnrm(cc, dc):
                    if dc == 0 and cc >= 1:
                        v.wait(("negC", cc - 1))
                    v.wait(("qev", cc, dc))
                    v.scalar_tensor_tensor(
                        xb2[:, dc % 2, 0:512], QT[:, qsl(cc, dc), :], 1.0,
                        QT[:, qsl(cc, dc), :], ALU.mult, ALU.mult)
                    v.mark(("sqb", cc, dc))
                    t.wait(("sqb", cc, dc))
                    t.wait(st2["mm7"])
                    t.matmul(mmp[:, 512:1024][0:16, 0:512],
                             qpat_s[:, dc, :], xb2[:, dc % 2, 0:512],
                             start=True, stop=True)
                    t.mark(("qnm2", cc, dc))
                    v.wait(("qnm2", cc, dc))
                    if dc == 0:
                        v.memset(qmax[:], 0.0)
                    v.reduce_max(sm[0:16, C_MQ:C_MQ + 1],
                                 mmp[:, 512:1024][0:16, 0:512], axis=AX.X)
                    v.mark(("qnrr", cc, dc))
                    v.wait(("qnrr", cc, dc))
                    v.tensor_max(qmax[:], qmax[:], sm[0:16, C_MQ:C_MQ + 1])
                    v.mark(("qnmx", cc, dc))
                    st2["mm7"] = ("qnmx", cc, dc)

                def emit_qtrans(cc, tt):
                    ti = 4 * cc + tt
                    if ti >= 2:
                        g.wait(("qxT", ti - 2))
                    if tt < 2:
                        g.wait(("qnm2", cc - 1, 6 + tt))
                    E.gdma(("xb2", ti), xb2[:, ti % 2, :],
                           x_hbm[128 * ti:128 * ti + 128, :])
                    t.wait(("xb2", ti))
                    t.wait(("c", 6))
                    t.wait(st2["mm6"])
                    for ch in range(8):
                        t.transpose(mmp_b[:, 128 * ch:128 * ch + 128],
                                    xb2[:, ti % 2, 128 * ch:128 * ch + 128],
                                    idn[:])
                    t.mark(("qxT", ti))
                    v.wait(("qxT", ti))
                    if tt == 0:
                        v.wait(("qmm8", cc - 1, 7))
                    v.tensor_copy(xTs4[:, :, 128 * tt:128 * tt + 128],
                                  mmp_b[:, 0:1024].rearrange(
                                      "p (c n) -> p c n", n=128))
                    v.mark(("qxTe", ti))
                    st2["mm6"] = ("qxTe", ti)

                def emit_proj_trans(cc, tt):
                    ti = 4 * cc + tt
                    t.wait(("pve", cc, 7))
                    t.wait(st2["mm6"])
                    for ch in range(8):
                        t.transpose(mmp_b[:, 128 * ch:128 * ch + 128],
                                    attO[:, tt, 128 * ch:128 * ch + 128],
                                    idn[:])
                    t.mark(("aT", ti))
                    v.wait(("aT", ti))
                    if ti >= 2:
                        v.wait(("pmm", ti - 2))
                    v.tensor_copy(aTs[:, ti % 2],
                                  mmp_b[:, 0:1024].rearrange(
                                      "p (c n) -> p c n", n=128))
                    v.mark(("aTe", ti))
                    st2["mm6"] = ("aTe", ti)

                def emit_proj_mm_a(cc, tt):
                    ti = 4 * cc + tt
                    t.wait(("aTe", ti))
                    t.wait(st2["mm6"])
                    t.wait(("wo",))
                    if ti >= 1:
                        t.wait(("y", ti - 1))
                    for ch in range(8):
                        t.matmul(mmp[:, 0:512],
                                 aTs[:, ti % 2, ch, :],
                                 wo[:, ch, 0:512],
                                 start=(ch == 0), stop=(ch == 7))

                def emit_proj_mm(cc, tt):
                    ti = 4 * cc + tt
                    t.wait(st2["mm7"])
                    for ch in range(8):
                        t.matmul(mmp[:, 512:1024],
                                 aTs[:, ti % 2, ch, :],
                                 wo[:, ch, 512:1024],
                                 start=(ch == 0), stop=(ch == 7))
                    t.mark(("pmm", ti))
                    st2["mm7"] = ("y", ti)
                    st2["mm6"] = ("y", ti)
                    v.wait(("pmm", ti))
                    v.wait(("xf", ti))
                    v.scalar_tensor_tensor(
                        xfy[:, ti % 2, :], mmp[:, 0:1024], 1.0,
                        xfy[:, ti % 2, :], ALU.mult, ALU.add)
                    v.mark(("y", ti))
                    pend_ln.append((emit_proj_ln, cc, tt))

                def emit_proj_ln(cc, tt):
                    ti = 4 * cc + tt
                    v.wait(("y", ti))
                    v.bn_stats(bns[:, 0:6], xfy[:, ti % 2, 0:512])
                    v.bn_stats(bns[:, 6:12], xfy[:, ti % 2, 512:1024])
                    v.mark(("ybn", ti))
                    v.wait(("ybn", ti))
                    v.bn_aggr(bns[:, 12:14], bns[:, 0:12])
                    v.mark(("yag", ti))
                    pend_ln2.append((emit_proj_ln2, cc, tt))

                def emit_proj_ln2(cc, tt):
                    ti = 4 * cc + tt
                    a.wait(("yag", ti))
                    a.activation(smc(C_LV), bns[:, 13:14], AF.Ln,
                                 bias=epsln_s[:])
                    a.mark(("lv", ti))
                    a.activation(smc(C_S1), bns[:, 13:14], AF.Ln)
                    a.mark(("lv0", ti))
                    a.wait(("lv", ti))
                    a.activation(smc(C_RSTD), smc(C_LV), AF.Exp,
                                 scale=-0.5)
                    a.mark(("rstd", ti))
                    a.wait(("lv0", ti))
                    a.activation(smc(C_S1), smc(C_S1), AF.Exp, scale=0.5)
                    a.mark(("e0", ti))
                    v.wait(("rstd", ti))
                    v.tensor_scalar(xfy[:, ti % 2, :], xfy[:, ti % 2, :],
                                    bns[:, 12:13], smc(C_RSTD),
                                    ALU.subtract, ALU.mult)
                    v.mark(("x1a", ti))
                    y.wait(("x1a", ti))
                    E.ydma(("spill", ti),
                           x1_hbm[128 * ti:128 * ti + 128, :],
                           xfy[:, ti % 2, :])
                    if ti + 2 < NT:
                        y.wait(("spill", ti))
                        E.ydma(("xf", ti + 2), xfy[:, ti % 2, :],
                               x_hbm[128 * (ti + 2):128 * (ti + 2) + 128,
                                     :])
                    v.wait(("e0", ti))
                    v.tensor_mul(smc(C_SS1), smc(C_S1), smc(C_RSTD))
                    v.mark(("s1", ti))
                    v.wait(("s1", ti))
                    v.tensor_scalar(smc(C_T1), smc(C_SS1), 1.0,
                                    EPS_RMS, ALU.mult, ALU.add)
                    v.mark(("t1", ti))
                    v.wait(("t1", ti))
                    v.reciprocal(r1_all[:, ti:ti + 1], smc(C_T1))
                    v.mark(("r1", ti))

                pend_in = []
                pend_end = []
                pend_ln = []
                pend_ln2 = []

                def drain_in():
                    if pend_in:
                        fn, cc, tt = pend_in.pop(0)
                        fn(cc, tt)

                def drain_end(p, budget=1):
                    n = 0
                    while pend_end and pend_end[0][3] <= p and n < budget:
                        fn, cc, tt, _ = pend_end.pop(0)
                        fn(cc, tt)
                        n += 1

                for c in range(NCH):
                    nj = 4 * c + 4
                    if c == 0:
                        for tt in range(4):
                            ti = tt
                            if ti >= 2:
                                g.wait(("qxT", ti - 2))
                            E.gdma(("xb2", ti), xb2[:, ti % 2, :],
                                   x_hbm[128 * ti:128 * ti + 128, :])
                            t.wait(("xb2", ti))
                            t.wait(("c", 6))
                            t.wait(st2["mm6"])
                            for ch in range(8):
                                t.transpose(
                                    mmp_b[:, 128 * ch:128 * ch + 128],
                                    xb2[:, ti % 2, 128 * ch:128 * ch + 128],
                                    idn[:])
                            t.mark(("qxT", ti))
                            v.wait(("qxT", ti))
                            v.tensor_copy(
                                xTs4[:, :, 128 * tt:128 * tt + 128],
                                mmp_b[:, 0:1024].rearrange(
                                    "p (c n) -> p c n", n=128))
                            v.mark(("qxTe", ti))
                            st2["mm6"] = ("qxTe", ti)
                    def emit_cbound(cc, tt=0):
                        v.wait(("qnmx", cc, 7))
                        v.tensor_copy(smb[0:16, 0:1], qmax[:])
                        v.mark(("qm16b", cc))
                        t.wait(("qm16b", cc))
                        t.wait(st2["mm7"])
                        t.transpose(mmp_b[0:1, 1536:1552], smb[0:16, 0:1],
                                    idn[0:16, 0:16])
                        t.mark(("qmT2", cc))
                        v.wait(("qmT2", cc))
                        v.reduce_max(sm[0:1, C_MQ:C_MQ + 1],
                                     mmp_b[0:1, 1536:1552], axis=AX.X)
                        v.mark(("mq2", cc))
                        v.wait(("mq2", cc))
                        v.wait(("mk2",))
                        v.tensor_mul(sm[0:1, C_C2:C_C2 + 1],
                                     sm[0:1, C_MQ:C_MQ + 1], mk2[:])
                        v.mark(("c2", cc))
                        a.wait(("c2", cc))
                        a.activation(sm[0:1, C_LV:C_LV + 1],
                                     sm[0:1, C_C2:C_C2 + 1], AF.Ln)
                        a.mark(("clv", cc))
                        a.wait(("clv", cc))
                        a.activation(sm[0:1, C_CS:C_CS + 1],
                                     sm[0:1, C_LV:C_LV + 1], AF.Exp,
                                     scale=0.5)
                        a.mark(("cs", cc))
                        t.wait(("cs", cc))
                        t.matmul(mmp[:, 544:545], ones_r[:],
                                 sm[0:1, C_CS:C_CS + 1],
                                 start=True, stop=True)
                        t.mark(("cb", cc))
                        if cc >= 2:
                            v.wait(("exp", cc - 2, 7,
                                    chunk_order[cc - 2][-1]))
                        v.wait(("cb", cc))
                        v.tensor_scalar(negC2[:, cc % 2:cc % 2 + 1],
                                        mmp[:, 544:545], -0.125, None,
                                        ALU.mult)
                        v.mark(("negC", cc))
                        st2["mm7"] = ("negC", cc)

                    if c == 0:
                        E.ydma(("xf", 0), xfy[:, 0, :], x_hbm[0:128, :])
                        E.ydma(("xf", 1), xfy[:, 1, :], x_hbm[128:256, :])
                        for dch in range(16):
                            emit_qtmm(0, dch)
                            if dch % 2 == 1:
                                emit_qev(0, dch // 2)
                                emit_qnrm(0, dch // 2)
                        emit_cbound(0)
                    if c + 1 < NCH:
                        pend_end += [(emit_qtrans, c + 1, tt, 0)
                                     for tt in range(4)]
                        for dc in range(8):
                            mp = max(0, dc - 1)
                            pend_end += [(emit_qtmm, c + 1, 2 * dc, mp),
                                         (emit_qtmm, c + 1, 2 * dc + 1, mp),
                                         (emit_qev, c + 1, dc, mp),
                                         (emit_qnrm, c + 1, dc, mp)]
                        pend_end += [(emit_cbound, c + 1, 0, 6)]

                    norder = list(range(nj))
                    chunk_order[c] = norder
                    pos_of = {jj: k for k, jj in enumerate(norder)}
                    start_pos = {s: min(pos_of[jj]
                                        for jj in range(4 * c + s + 1))
                                 for s in range(4)}
                    stop_pos = {s: max(pos_of[jj]
                                       for jj in range(4 * c + s + 1))
                                for s in range(4)}

                    def prev_key(p, pos):
                        if pos >= 2:
                            return (c, p, norder[pos - 2])
                        if p >= 1:
                            return (c, p - 1, norder[nj - 2 + pos])
                        if c >= 1:
                            po = chunk_order[c - 1]
                            return (c - 1, 7, po[len(po) - 2 + pos])
                        return None

                    if True:
                        def emit_scores(p, pos):
                            j = norder[pos]
                            slot = pos % 2
                            r = j - 4 * c
                            off = 128 * r if r > 0 else 0
                            nvalid = 512 - off
                            scp = scs[slot]
                            t.wait(("qev", c, 7))
                            pk = prev_key(p, pos)
                            if pk is not None:
                                t.wait(("exp",) + pk)
                            t.matmul(scp[:, off:512],
                                     KT[0:64, p, 128 * j:128 * j + 128],
                                     QT[0:64, qsl(c, p), off:512],
                                     start=True, stop=True,
                                     tile_position=(0, 0))
                            t.matmul(scp[:, 512 + off:1024],
                                     KT[64:128, p, 128 * j:128 * j + 128],
                                     QT[64:128, qsl(c, p), off:512],
                                     start=True, stop=True,
                                     tile_position=(64, 0))
                            t.mark(("smm", c, p, j))
                            a.wait(("smm", c, p, j))
                            if pk is not None:
                                a.wait(("pv",) + pk)
                            a.wait(("negC", c))
                            a.activation(
                                bass.AP(p_sb, 1024 * slot + off,
                                        [[2048, 128], [512, 2],
                                         [1, nvalid]]),
                                bass.AP(scp, off,
                                        [[1024, 128], [512, 2],
                                         [1, nvalid]]),
                                AF.Exp, bias=negC2[:, c % 2:c % 2 + 1],
                                scale=0.125)
                            a.mark(("exp", c, p, j))
                            if r >= 0:
                                e_m = g if p == 0 else v
                                e_m.wait(("exp", c, p, j))
                                for h in range(2):
                                    mp = bass.AP(p_sb,
                                                 1024 * slot + 512 * h + off,
                                                 [[2048, 128], [1, 128]])
                                    if p == 0:
                                        e_m.tensor_tensor(mp, mp, tri_s[:],
                                                          ALU.mult)
                                    else:
                                        e_m.tensor_mul(mp, mp, tri_s[:])
                                e_m.mark(("msk", c, p, j))

                        def emit_pv(p, pos):
                            j = norder[pos]
                            slot = pos % 2
                            r = j - 4 * c
                            if r >= 0:
                                t.wait(("msk", c, p, j))
                            else:
                                t.wait(("exp", c, p, j))
                            for h in range(2):
                                if pos == 0:
                                    if p >= 1:
                                        t.wait(("pveh", c, p - 1, h))
                                    elif c >= 1:
                                        t.wait(("pveh", c - 1, 7, h))
                                ten, base, pstr = pv_even[h]
                                for s in range(max(0, r), 4):
                                    t.matmul(
                                        bass.AP(ten, base + 66 * s,
                                                [[pstr, 128], [1, 65]]),
                                        p_sb[:, slot,
                                             512 * h + 128 * s:
                                             512 * h + 128 * s + 128],
                                        VS[:, j, 2 * p + h, 0:65],
                                        start=(pos == 0 and s == 0),
                                        stop=(pos == stop_pos[s]),
                                        skip_group_check=True)
                            t.mark(("pv", c, p, j))

                        def emit_epilogue(p):
                            v.wait(("pv", c, p, norder[-1]))
                            if c >= 1:
                                v.wait(("aT", 4 * (c - 1) + 3))
                            for h in range(2):
                                ten, base, pstr = pv_even[h]
                                v.reciprocal(recs[:, 4 * h:4 * h + 4],
                                             bass.AP(ten, base + 64,
                                                     [[pstr, 128], [66, 4]]))
                                v.mark(("rc", c, p, h))
                                v.wait(("rc", c, p, h))
                                for s in range(4):
                                    v.tensor_scalar(
                                        attO[:, s, 128 * p + 64 * h:
                                             128 * p + 64 * h + 64],
                                        bass.AP(ten, base + 66 * s,
                                                [[pstr, 128], [1, 64]]),
                                        recs[:, 4 * h + s:4 * h + s + 1],
                                        None, ALU.mult)
                                v.mark(("pveh", c, p, h))
                            v.mark(("pve", c, p), inc=0)
                            while (pend_end and pend_end[0][0] is emit_qev
                                   and pend_end[0][3] <= p):
                                fn, cc2, tt2, _ = pend_end.pop(0)
                                fn(cc2, tt2)
                            while pend_in:
                                drain_in()

                        seq = [(p, k) for p in range(8) for k in range(nj)]
                        for gx, (p, k) in enumerate(seq):
                            if k == 2 and pend_ln:
                                fn, cc2, tt2 = pend_ln.pop(0)
                                fn(cc2, tt2)
                            if k == 4 and pend_ln2:
                                fn, cc2, tt2 = pend_ln2.pop(0)
                                fn(cc2, tt2)
                            if p == 0 and k >= 2:
                                drain_in()
                                if nj <= 8:
                                    drain_in()
                            elif p >= 1 and k >= 5 and (k % 2 == 1
                                                        or nj <= 16):
                                drain_end(p - 1,
                                          budget=(2 if nj <= 12 else 1))
                            emit_scores(p, k)
                            if gx >= 2:
                                p2, k2 = seq[gx - 2]
                                emit_pv(p2, k2)
                                if k2 == nj - 1:
                                    emit_epilogue(p2)
                        for gx in (len(seq) - 2, len(seq) - 1):
                            p2, k2 = seq[gx]
                            emit_pv(p2, k2)
                            if k2 == nj - 1:
                                emit_epilogue(p2)
                    while pend_end:
                        drain_end(99)
                    pend_in = [(emit_proj_trans, c, 0),
                               (emit_proj_trans, c, 1),
                               (emit_proj_mm_a, c, 0),
                               (emit_proj_mm, c, 0),
                               (emit_proj_trans, c, 2),
                               (emit_proj_mm_a, c, 1),
                               (emit_proj_mm, c, 1),
                               (emit_proj_trans, c, 3)]
                    pend_end = [(emit_proj_mm_a, c, 2, 0),
                                (emit_proj_mm, c, 2, 0),
                                (emit_proj_mm_a, c, 3, 1),
                                (emit_proj_mm, c, 3, 1)]

                while pend_in:
                    drain_in()
                while pend_end or pend_ln or pend_ln2:
                    while pend_ln or pend_ln2:
                        if pend_ln:
                            fn, cc2, tt2 = pend_ln.pop(0)
                            fn(cc2, tt2)
                        while pend_ln2:
                            fn, cc2, tt2 = pend_ln2.pop(0)
                            fn(cc2, tt2)
                    if pend_end:
                        drain_end(99)

                E.y.wait(("spill", NT - 1))
                E.y.wait(("spill", NT - 2))

            def trx_slice(mb):
                return mb[0:1, 512:528]

            _emit(nc, "phase2", phase2, P)

        nc.all_engine_barrier()
        mctx.close()

        with ExitStack() as ctx:
            def sb(name, shape, dtype):
                return ctx.enter_context(nc.sbuf_tensor(name, shape, dtype))

            wupT = sb("wupT", [128, 8, FF], BF16)
            wdnT = sb("wdnT", [128, 32, D], BF16)
            wr3 = sb("wr3", [128, 3, D], F32)
            m1b = sb("m1b", [128, 2, D], BF16)
            m2b = sb("m2b", [128, 1, D], BF16)
            absw = sb("absw", [128, D], F32)
            wtb = sb("wtb", [128, 2, D], BF16)
            x1b = sb("x1b", [128, 2, D], BF16)
            x1f = sb("x1f", [128, 2, D], F32)
            x1Ts = sb("x1Ts", [128, 2, 8, 128], BF16)
            h_sb = sb("h_sb", [128, 2, FF], BF16)
            hTs = sb("hTs", [128, 32, 128], BF16)
            outs = sb("outs", [128, 2, D], F32)
            sqd3 = sb("sqd3", [128, 2048], BF16)

            upq = [ctx.enter_context(
                nc.psum_tensor(f"upq{i}", [128, 512], F32))
                for i in range(2)]
            tra = ctx.enter_context(nc.psum_tensor("tra", [128, 512], F32))
            trb = ctx.enter_context(nc.psum_tensor("trb", [128, 512], F32))
            dwn2 = [ctx.enter_context(
                nc.psum_tensor(f"dwn{i}", [128, 1024], F32))
                for i in range(2)]
            tra_b = tra.bitcast(BF16)
            trb_b = trb.bitcast(BF16)

            def phase3(E):
                t, v, a, g, y = E.t, E.v, E.a, E.g, E.y
                a.activation(smc(C_SD), ones_c[:], AF.Gelu)
                a.mark(("apin3",))

                state = {"tra": None, "upq": [None, None]}

                def tern_chunk(r):
                    if r >= 3:
                        y.wait(("tern", r - 3))
                    if r < 32:
                        src = wup_hbm[128 * r:128 * r + 128, :]
                        thr = thr4[:, 0:1]
                    else:
                        rr = r - 32
                        qq, dc = rr // 8, rr % 8
                        src = wdn_hbm[128 * dc:128 * dc + 128,
                                      1024 * qq:1024 * qq + 1024]
                        thr = thr4[:, 1:2]
                    E.ydma(("w3", r), wr3[:, r % 3, :], src)
                    a.wait(("w3", r))
                    a.wait(("apin3",))
                    if r >= 2:
                        a.wait(("tern", r - 2))
                    a.activation(m1b[:, r % 2, :], wr3[:, r % 3, :], AF.Sign)
                    a.mark(("sgn", r))
                    if r >= 1:
                        a.wait(("mskt", r - 1))
                    a.activation(absw[:], wr3[:, r % 3, :], AF.Abs)
                    a.mark(("absw", r))
                    v.wait(("absw", r))
                    if r >= 2:
                        v.wait(("tern", r - 2))
                    v.tensor_scalar(m2b[:, 0, :], absw[:],
                                    thr, None, ALU.is_gt)
                    v.mark(("mskt", r))
                    v.wait(("mskt", r))
                    v.wait(("sgn", r))
                    if r >= 2:
                        v.wait(("wTr", r - 2))
                    v.tensor_mul(wtb[:, r % 2, :], m1b[:, r % 2, :],
                                 m2b[:, 0, :])
                    v.mark(("tern", r))
                    t.wait(("tern", r))
                    t.wait(state["tra"])
                    for ch in range(8):
                        t.transpose(tra_b[:, 128 * ch:128 * ch + 128],
                                    wtb[:, r % 2, 128 * ch:128 * ch + 128],
                                    idn[:])
                    t.mark(("wTr", r))
                    v.wait(("wTr", r))
                    if r < 32:
                        dst = wupT[:, :, 128 * r:128 * r + 128]
                    else:
                        rr = r - 32
                        qq, dc = rr // 8, rr % 8
                        dst = wdnT[:, 8 * qq:8 * qq + 8,
                                   128 * dc:128 * dc + 128]
                    v.tensor_copy(dst, tra_b[:, 0:1024].rearrange(
                        "p (c n) -> p c n", n=128))
                    v.mark(("wTe", r))
                    state["tra"] = ("wTe", r)

                upq_hist = []

                def issue_x1b(i):
                    g.wait(("spill", i))
                    if i >= 2:
                        g.wait(("x1T", i - 2))
                    E.gdma(("x1b", i), x1b[:, i % 2, :],
                           x1_hbm[128 * i:128 * i + 128, :])

                def up_trans(i):
                    t.wait(("x1b", i))
                    t.wait(state["tra"])
                    for ch in range(8):
                        t.transpose(tra_b[:, 128 * ch:128 * ch + 128],
                                    x1b[:, i % 2, 128 * ch:128 * ch + 128],
                                    idn[:])
                    t.mark(("x1T", i))
                    v.wait(("x1T", i))
                    if i >= 2:
                        v.wait(("up", i - 2, 7))
                    v.tensor_copy(x1Ts[:, i % 2], tra_b[:, 0:1024].rearrange(
                        "p (c n) -> p c n", n=128))
                    v.mark(("x1Te", i))
                    state["tra"] = ("x1Te", i)
                    if i >= 2:
                        y.wait(("o", i - 2))
                    y.wait(("spill", i))
                    E.ydma(("x1f", i), x1f[:, i % 2, :],
                           x1_hbm[128 * i:128 * i + 128, :])

                def up_part_q(i, q):
                    if q == 0:
                        t.wait(("x1Te", i))
                    t.wait(("wTe", 4 * q + 3))
                    bank = len(upq_hist) % 2
                    if len(upq_hist) >= 2:
                        t.wait(("gel",) + upq_hist[-2])
                    upq_hist.append((i, q))
                    for ch in range(8):
                        t.matmul(upq[bank][:, 0:512],
                                 x1Ts[:, i % 2, ch, :],
                                 wupT[:, ch, 512 * q:512 * q + 512],
                                 start=(ch == 0), stop=(ch == 7))
                    t.mark(("up", i, q))
                    a.wait(("up", i, q))
                    if i >= 2:
                        a.wait(("hT", i - 2, q // 2))
                    a.activation(h_sb[:, i % 2, 512 * q:512 * q + 512],
                                 upq[bank][:, 0:512], AF.Gelu,
                                 scale=r1_all[:, i:i + 1])
                    a.mark(("gel", i, q))

                def up_part(i):
                    for q in range(8):
                        up_part_q(i, q)

                def down_part(i):
                    def hround(k):
                        t.wait(("gel", i, 2 * k + 1))
                        bb = (trb_b, tra_b)[k % 2]
                        if k % 2 == 0:
                            if i >= 1 or k >= 2:
                                pk = (i, k - 2) if k >= 2 else (i - 1, 2)
                                t.wait(("hTe",) + pk)
                        else:
                            t.wait(state["tra"])
                        for u in range(8):
                            t.transpose(bb[:, 128 * u:128 * u + 128],
                                        h_sb[:, i % 2,
                                             1024 * k + 128 * u:
                                             1024 * k + 128 * u + 128],
                                        idn[:])
                        t.mark(("hT", i, k))
                        ev = a if k % 2 == 0 else v
                        ev.wait(("hT", i, k))
                        ev_src = bb[:, 0:1024].rearrange(
                            "p (c n) -> p c n", n=128)
                        if k % 2 == 0:
                            ev.activation(hTs[:, 8 * k:8 * k + 8, :],
                                          ev_src, AF.Copy)
                        else:
                            ev.tensor_copy(hTs[:, 8 * k:8 * k + 8, :],
                                           ev_src)
                        ev.mark(("hTe", i, k))
                        if k % 2 == 1:
                            state["tra"] = ("hTe", i, k)

                    def dround(k):
                        if k == 0 and i >= 2:
                            t.wait(("y2", i - 2))
                        t.wait(("hTe", i, k))
                        t.wait(("wTe", 32 + 8 * k + 7))
                        for fc in range(8 * k, 8 * k + 8):
                            for nb in range(2):
                                t.matmul(
                                    dwn2[i % 2][:, 512 * nb:512 * nb + 512],
                                    hTs[:, fc, :],
                                    wdnT[:, fc, 512 * nb:512 * nb + 512],
                                    start=(fc == 0), stop=(fc == 31))

                    hround(0)
                    hround(1)
                    dround(0)
                    hround(2)
                    dround(1)
                    hround(3)
                    dround(2)
                    dround(3)
                    t.mark(("dn", i))
                    a.wait(("gel", i, 7))
                    if i >= 1:
                        a.wait(("y2ss", i - 1))
                    a.activation(sqd3[:], h_sb[:, i % 2, 0:2048], AF.Square,
                                 accum_out=smc(C_SS2))
                    a.mark(("hssa", i))
                    a.wait(("hssa", i))
                    a.activation(sqd3[:], h_sb[:, i % 2, 2048:4096],
                                 AF.Square, accum_out=sm[:, 22:23])
                    a.mark(("hss", i))
                    v.wait(("hss", i))
                    v.tensor_tensor(smc(C_SS2), smc(C_SS2), sm[:, 22:23],
                                    ALU.add)
                    v.mark(("hsum", i))
                    g.wait(("hsum", i))
                    g.tensor_tensor(smc(C_S2), smc(C_SS2), half_s[:],
                                    ALU.pow)
                    g.mark(("s2", i))
                    v.wait(("s2", i))
                    v.tensor_scalar(smc(C_T1), smc(C_S2), 1.0 / 64.0,
                                    EPS_RMS, ALU.mult, ALU.add)
                    v.mark(("t2", i))
                    v.wait(("t2", i))
                    v.reciprocal(smc(C_R2), smc(C_T1))
                    v.mark(("r2", i))
                    v.wait(("dn", i))
                    v.wait(("x1f", i))
                    v.wait(("r2", i))
                    v.scalar_tensor_tensor(
                        x1f[:, i % 2, :], dwn2[i % 2][:, 0:1024], smc(C_R2),
                        x1f[:, i % 2, :], ALU.mult, ALU.add,
                        accum_out=smc(C_YSUM))
                    v.mark(("y2", i))
                    a.wait(("y2", i))
                    a.activation(sqd3[:, 0:D], x1f[:, i % 2, :], AF.Square,
                                 accum_out=smc(C_YSS))
                    a.mark(("y2ss", i))
                    v.wait(("y2", i))
                    v.tensor_scalar(smc(C_M), smc(C_YSUM), 1.0 / D, None,
                                    ALU.mult)
                    v.mark(("m2", i))
                    a.wait(("m2", i))
                    a.activation(smc(C_MSQ), smc(C_M), AF.Square)
                    a.mark(("msq2", i))
                    v.wait(("y2ss", i))
                    v.wait(("msq2", i))
                    v.tensor_scalar(smc(C_VAR), smc(C_YSS), 1.0 / D,
                                    smc(C_MSQ), ALU.mult, ALU.subtract)
                    v.mark(("var2", i))
                    v.wait(("var2", i))
                    v.tensor_scalar(smc(C_VE), smc(C_VAR), EPS_LN, None,
                                    ALU.add)
                    v.mark(("ve2", i))
                    g.wait(("ve2", i))
                    g.tensor_tensor(smc(C_SD), smc(C_VE), half_s[:],
                                    ALU.pow)
                    g.mark(("sd2", i))
                    v.wait(("sd2", i))
                    v.reciprocal(smc(C_RSTD), smc(C_SD))
                    v.mark(("rstd2", i))
                    v.wait(("rstd2", i))
                    if i >= 2:
                        v.wait(("oD", i - 2))
                    v.tensor_scalar(outs[:, i % 2, :], x1f[:, i % 2, :],
                                    smc(C_M), smc(C_RSTD), ALU.subtract,
                                    ALU.mult)
                    v.mark(("o", i))
                    y.wait(("o", i))
                    E.ydma(("oD", i), out_hbm[128 * i:128 * i + 128, :],
                           outs[:, i % 2, :])

                for r in range(32):
                    tern_chunk(r)
                issue_x1b(0)
                up_trans(0)
                up_part(0)
                issue_x1b(1)
                up_trans(1)
                up_part(1)
                issue_x1b(2)
                for r in range(32, 64):
                    tern_chunk(r)
                for i in range(NT):
                    if i + 3 < NT:
                        issue_x1b(i + 3)
                    if i + 2 < NT:
                        up_trans(i + 2)
                    down_part(i)
                    if i + 2 < NT:
                        up_part(i + 2)
                y.wait(("oD", NT - 2))
                y.wait(("oD", NT - 1))

            _emit(nc, "phase3", phase3, P)

    return nc


_NC_CACHE = {}


def kernel(L0_attn_x, L0_attn_w_qkv, L0_attn_w_out, L0_ffn_w_up,
           L0_ffn_w_down):
    x = np.ascontiguousarray(np.asarray(L0_attn_x, dtype=np.float32))
    wqkv = np.ascontiguousarray(np.asarray(L0_attn_w_qkv, dtype=np.float32))
    wout = np.ascontiguousarray(np.asarray(L0_attn_w_out, dtype=np.float32))
    wup = np.ascontiguousarray(np.asarray(L0_ffn_w_up, dtype=np.float32))
    wdn = np.ascontiguousarray(np.asarray(L0_ffn_w_down, dtype=np.float32))
    B = x.shape[0]
    if "nc" not in _NC_CACHE:
        _NC_CACHE["nc"] = build_program()
    nc = _NC_CACHE["nc"]
    in_maps = [{"x": np.ascontiguousarray(x[b]), "w_qkv": wqkv,
                "w_out": wout, "w_up": wup, "w_down": wdn}
               for b in range(B)]
    res = run_bass_kernel_spmd(nc, in_maps, core_ids=list(range(B)))
    out = np.stack([np.asarray(res.results[b]["out"]) for b in range(B)],
                   axis=0)
    return out.astype(np.float32)


# version 35
# speedup vs baseline: 1.0222x; 1.0131x over previous
import numpy as np
import ml_dtypes
from contextlib import ExitStack

import concourse.bass as bass
import concourse.mybir as mybir
from concourse.bass_utils import run_bass_kernel_spmd

F32 = mybir.dt.float32
BF16 = mybir.dt.bfloat16
AF = mybir.ActivationFunctionType
ALU = mybir.AluOpType
AX = mybir.AxisListType

S = 4096
D = 1024
H = 16
FF = 4096
NT = S // 128
NCH = S // 512
EPS_LN = 1e-5
EPS_RMS = 1e-8
ENGS = ("t", "v", "a", "g", "y")


class _Eng:

    def __init__(self, prog, name):
        object.__setattr__(self, "prog", prog)
        object.__setattr__(self, "name", name)
        object.__setattr__(self, "real", None)
        object.__setattr__(self, "last", None)

    def __getattr__(self, op):
        def fn(*args, **kw):
            if self.real is not None:
                object.__setattr__(self, "last",
                                   getattr(self.real, op)(*args, **kw))
            return self.last
        return fn

    def mark(self, key, inc=1):
        prog = self.prog
        if prog.planning:
            prog.ctr[self.name] += inc
            assert key not in prog.marks, f"dup mark {key}"
            prog.marks[key] = (self.name, prog.ctr[self.name])
        elif self.real is not None and inc:
            self.last.then_inc(prog.sems[self.name], inc)

    def wait(self, key):
        prog = self.prog
        if prog.planning or key is None or self.real is None:
            return
        src, val = prog.marks[key]
        seen = prog.waited.get((self.name, src), 0)
        if val <= seen:
            return
        prog.waited[(self.name, src)] = val
        self.real.wait_ge(prog.sems[src], val)


DMA_STREAMS = {"c": 1, "wkv": 1, "wq": 1, "wo": 1, "xb": 2, "aw": 2,
               "xb2": 2, "xf": 2, "spill": 2, "w3": 3, "x1b": 2,
               "x1f": 2, "oD": 2}


class _Prog:
    def __init__(self):
        self.marks = {}
        self.ctr = {n: 0 for n in ENGS}
        self.planning = True
        self.sems = {}
        self.waited = {}
        self.snum = {}
        for n in ENGS:
            setattr(self, n, _Eng(self, n))

    def _dma(self, eng, stream, key, out, in_):
        ring = DMA_STREAMS[stream]
        if self.planning:
            o = self.snum.get(stream, 0)
            self.snum[stream] = o + 1
            sname = f"q_{stream}{o % ring}"
            assert key not in self.marks, key
            self.marks[key] = (sname, 16 * (o // ring + 1))
        if eng.real is not None:
            ins = eng.real.dma_start(out, in_)
            object.__setattr__(eng, "last", ins)
            sname, _ = self.marks[key]
            ins.then_inc(self.sems[sname], 16)

    def gdma(self, key, out, in_, stream=None):
        self._dma(self.g, stream or key[0], key, out, in_)

    def ydma(self, key, out, in_, stream=None):
        self._dma(self.y, stream or key[0], key, out, in_)


def _emit(nc, block_name, sched, P):
    P.planning = True
    sched(P)
    P.planning = False
    with nc.Block(block_name) as block:
        def run(live, real):
            for n in ENGS:
                object.__setattr__(getattr(P, n), "real", None)
            object.__setattr__(getattr(P, live), "real", real)
            P.waited = {}
            sched(P)
            object.__setattr__(getattr(P, live), "real", None)
        block.tensor(lambda e: run("t", e))
        block.vector(lambda e: run("v", e))
        block.scalar(lambda e: run("a", e))
        block.gpsimd(lambda e: run("g", e))
        block.sync(lambda e: run("y", e))


def build_program():
    nc = bass.Bass("TRN2", target_bir_lowering=False)

    x_hbm = nc.dram_tensor("x", [S, D], F32, kind="ExternalInput")
    wqkv_hbm = nc.dram_tensor("w_qkv", [D, 3 * D], F32, kind="ExternalInput")
    wout_hbm = nc.dram_tensor("w_out", [D, D], F32, kind="ExternalInput")
    wup_hbm = nc.dram_tensor("w_up", [FF, D], F32, kind="ExternalInput")
    wdn_hbm = nc.dram_tensor("w_down", [D, FF], F32, kind="ExternalInput")
    out_hbm = nc.dram_tensor("out", [S, D], F32, kind="ExternalOutput")
    x1_hbm = nc.dram_tensor("x1spill", [S, D], F32)

    idn_h = nc.inline_tensor(np.eye(128, dtype=ml_dtypes.bfloat16), "idn")
    tri = np.triu(np.ones((128, 128), np.float32)).astype(ml_dtypes.bfloat16)
    tri_h = nc.inline_tensor(tri, "tri2")
    ones_r_h = nc.inline_tensor(np.ones((1, 128), np.float32), "ones_r")
    ones_c_h = nc.inline_tensor(np.ones((128, 1), np.float32), "ones_c")
    half_h = nc.inline_tensor(np.full((128, 1), 0.5, np.float32), "half")
    qpat_np = np.zeros((128, 8, 16), np.float32)
    for ch in range(8):
        qpat_np[0:64, ch, 2 * ch] = 1.0
        qpat_np[64:128, ch, 2 * ch + 1] = 1.0
    qpat_h = nc.inline_tensor(qpat_np.astype(ml_dtypes.bfloat16), "qpat")
    epsln_h = nc.inline_tensor(np.full((128, 1), EPS_LN, np.float32), "epsln")

    P = _Prog()

    with ExitStack() as octx:
        def osb(name, shape, dtype):
            return octx.enter_context(nc.sbuf_tensor(name, shape, dtype))

        for n in ENGS:
            P.sems[n] = octx.enter_context(nc.semaphore("sem_" + n))
        for sname, ring in DMA_STREAMS.items():
            for i in range(ring):
                P.sems[f"q_{sname}{i}"] = octx.enter_context(
                    nc.semaphore(f"q_{sname}{i}"))

        r1_all = osb("r1_all", [128, NT], F32)
        idn = osb("idn_s", [128, 128], BF16)
        ones_r = osb("ones_r_s", [1, 128], F32)
        ones_c = osb("ones_c_s", [128, 1], F32)
        half_s = osb("half_s", [128, 1], F32)
        epsln_s = osb("epsln_s", [128, 1], F32)
        qpat_s = osb("qpat_s", [128, 8, 16], BF16)
        thr4 = osb("thr4", [128, 4], F32)
        sm = osb("sm", [128, 23], F32)
        bns = osb("bns", [128, 14], F32)
        mctx = octx.enter_context(ExitStack())

        def msb(name, shape, dtype):
            return mctx.enter_context(nc.sbuf_tensor(name, shape, dtype))

        KT = msb("KT", [128, 8, S], BF16)
        VS = msb("VS", [128, NT, H, 65], BF16)
        tri_s = msb("tri_s", [128, 128], BF16)
        mk2 = msb("mk2", [1, 1], F32)
        smb = msb("smb", [128, 16], BF16)

        (C_AU, C_AD, C_TOT, C_TOT2, C_MQ, C_C2, C_CS, C_M, C_MSQ, C_VAR,
         C_LV, C_RSTD, C_SS1, C_S1, C_T1, C_YSUM, C_YSS, C_SS2, C_S2,
         C_R2, C_VE, C_SD) = range(22)

        def smc(i, n=1):
            return sm[:, i:i + n]

        with ExitStack() as ctx:
            def sb(name, shape, dtype):
                return ctx.enter_context(nc.sbuf_tensor(name, shape, dtype))

            wkv = sb("wkv", [128, 8, 2 * D], BF16)
            sm16 = sb("sm16", [128, 16], F32)
            xb1 = sb("xb1", [128, 2, D], BF16)
            xTs = sb("xTs", [128, 2, 8, 128], BF16)
            kbuf = sb("kbuf", [128, 2, D], BF16)
            sqK = sb("sqK", [128, D], BF16)
            sqA = sb("sqA", [128, D], BF16)
            wr1 = sb("wr1", [128, 2, D], F32)
            aasU = sb("aasU", [128, 32], F32)
            aasD = sb("aasD", [128, 32], F32)
            kmax = sb("kmax", [128, H], F32)

            kvp = ctx.enter_context(nc.psum_tensor("kvp", [128, 2048], F32))
            trp = ctx.enter_context(nc.psum_tensor("trp", [128, 512], F32))
            trq = ctx.enter_context(nc.psum_tensor("trq", [128, 512], F32))
            tru = ctx.enter_context(nc.psum_tensor("tru", [128, 512], F32))
            trv = ctx.enter_context(nc.psum_tensor("trv", [128, 512], F32))
            trp_b = trp.bitcast(BF16)
            trq_b = trq.bitcast(BF16)
            tru_b = tru.bitcast(BF16)
            trv_b = trv.bitcast(BF16)

            def phase1(E):
                t, v, a, g, y = E.t, E.v, E.a, E.g, E.y
                E.ydma(("c", 0), idn[:], idn_h[:])
                E.ydma(("c", 1), tri_s[:], tri_h[:])
                E.ydma(("c", 2), ones_r[:], ones_r_h[:])
                E.ydma(("c", 3), ones_c[:], ones_c_h[:])
                E.ydma(("c", 4), half_s[:], half_h[:])
                E.ydma(("c", 5), epsln_s[:], epsln_h[:])
                E.ydma(("c", 6), qpat_s[:], qpat_h[:])
                E.gdma(("wkv",), wkv[:],
                       wqkv_hbm[:, D:3 * D].rearrange("(c p) n -> p c n",
                                                      p=128))
                v.memset(kmax[:], 0.0)
                v.mark(("km0",))
                v.memset(VS[:], 1.0)
                v.mark(("vs1",))
                a.wait(("c", 6))
                a.activation(smc(C_SD), ones_c[:], AF.Exp)
                a.mark(("apin",))

                def k_trans(i):
                    t.wait(("ke", i))
                    if i >= 2:
                        t.wait(("kte", i - 2))
                    ktb = (tru_b, trv_b)[i % 2]
                    for c in range(8):
                        t.transpose(ktb[:, 128 * c:128 * c + 128],
                                    kbuf[:, i % 2, 128 * c:128 * c + 128],
                                    idn[:])
                    t.mark(("ktr", i))
                    v.wait(("ktr", i))
                    v.tensor_copy(KT[:, :, 128 * i:128 * i + 128],
                                  ktb[:, 0:1024].rearrange(
                                      "p (c n) -> p c n", n=128))
                    v.mark(("kte", i))

                for i in range(NT):
                    if i >= 2:
                        g.wait(("xT", i - 2))
                    E.gdma(("xb", i), xb1[:, i % 2, :],
                           x_hbm[128 * i:128 * i + 128, :])
                    t.wait(("xb", i))
                    t.wait(("c", 6))
                    if i >= 2:
                        t.wait(("xTe", i - 2))
                    xtb = (trp_b, trq_b)[i % 2]
                    for c in range(8):
                        t.transpose(xtb[:, 128 * c:128 * c + 128],
                                    xb1[:, i % 2, 128 * c:128 * c + 128],
                                    idn[:])
                    t.mark(("xT", i))
                    v.wait(("xT", i))
                    if i >= 2:
                        v.wait(("kv", i - 2))
                    v.tensor_copy(xTs[:, i % 2],
                                  xtb[:, 0:1024].rearrange(
                                      "p (c n) -> p c n", n=128))
                    v.mark(("xTe", i))
                    if i >= 1:
                        k_trans(i - 1)
                    t.wait(("xTe", i))
                    t.wait(("wkv",))
                    for nb in range(4):
                        if nb == 0 and i >= 1:
                            t.wait(("ke", i - 1))
                        if nb == 2 and i >= 1:
                            t.wait(("ve", i - 1))
                        for c in range(8):
                            t.matmul(kvp[:, 512 * nb:512 * nb + 512],
                                     xTs[:, i % 2, c, :],
                                     wkv[:, c, 512 * nb:512 * nb + 512],
                                     start=(c == 0), stop=(c == 7))
                        if nb == 1:
                            t.mark(("kvK", i))
                    t.mark(("kv", i))
                    a.wait(("kvK", i))
                    if i >= 2:
                        a.wait(("ktr", i - 2))
                        a.wait(("knr", i - 2))
                    a.activation(kbuf[:, i % 2, :], kvp[:, 0:D], AF.Copy)
                    a.mark(("ke", i))
                    a.wait(("kv", i))
                    a.wait(("vs1",))
                    a.activation(VS[:, i, :, 0:64],
                                 kvp[:, D:2 * D].rearrange(
                                     "p (h e) -> p h e", e=64), AF.Copy)
                    a.mark(("ve", i))
                    for ii in ([i - 1] if i >= 1 else []) + \
                            ([i] if i == NT - 1 else []):
                        v.wait(("ve", ii))
                        v.scalar_tensor_tensor(sqK[:], kbuf[:, ii % 2, :],
                                               1.0, kbuf[:, ii % 2, :],
                                               ALU.mult, ALU.mult)
                        v.mark(("ksq", ii))
                        v.wait(("ksq", ii))
                        v.reduce_sum(sm16[:], sqK[:].rearrange(
                            "p (h e) -> p h e", e=64), axis=AX.X)
                        v.mark(("knr", ii))
                        v.wait(("knr", ii))
                        v.wait(("km0",))
                        v.tensor_max(kmax[:], kmax[:], sm16[:])
                        v.mark(("kmx", ii))
                    for r in (2 * i, 2 * i + 1):
                        if r >= 2:
                            y.wait(("aact", r - 2))
                        if r < 32:
                            src_ap = wup_hbm[128 * r:128 * r + 128, :]
                        else:
                            rr = r - 32
                            src_ap = wdn_hbm[
                                128 * (rr // 4):128 * (rr // 4) + 128,
                                1024 * (rr % 4):1024 * (rr % 4) + 1024]
                        E.ydma(("aw", r), wr1[:, r % 2, :], src_ap)
                        a.wait(("aw", r))
                        if r >= 1:
                            a.wait(("aact", r - 1))
                        acc = (aasU[:, r:r + 1] if r < 32
                               else aasD[:, r - 32:r - 31])
                        a.activation(sqA[:], wr1[:, r % 2, :], AF.Abs,
                                     accum_out=acc)
                        a.mark(("aact", r))

                k_trans(NT - 1)

                v.wait(("aact", 31))
                v.reduce_sum(smc(C_AU), aasU[:], axis=AX.X)
                v.mark(("redU",))
                v.wait(("aact", 63))
                v.reduce_sum(smc(C_AD), aasD[:], axis=AX.X)
                v.mark(("redD",))
                t.wait(("redU",))
                t.wait(("redD",))
                t.wait(("c", 6))
                t.wait(("ve", NT - 1))
                t.matmul(kvp[0:1, 0:1], smc(C_AU), ones_c[:],
                         start=True, stop=True)
                t.matmul(kvp[0:1, 1:2], smc(C_AD), ones_c[:],
                         start=True, stop=True)
                t.mark(("atot",))
                v.wait(("atot",))
                v.tensor_scalar(sm[0:1, C_TOT:C_TOT + 2], kvp[0:1, 0:2],
                                0.5 / float(FF * D), None, ALU.mult)
                v.mark(("tot01",))
                t.wait(("tot01",))
                t.matmul(kvp[:, 2:4], ones_r[:], sm[0:1, C_TOT:C_TOT + 2],
                         start=True, stop=True)
                t.mark(("thrb",))
                v.wait(("thrb",))
                v.tensor_copy(thr4[:, 0:2], kvp[:, 2:4])
                v.mark(("thr",))
                v.wait(("thr",))
                v.tensor_scalar(thr4[:, 2:4], thr4[:, 0:2], -1.0, None,
                                ALU.mult)
                v.mark(("thrn",))

                v.wait(("kmx", NT - 1))
                v.tensor_copy(smb[:], kmax[:])
                v.mark(("kmb",))
                t.wait(("kmb",))
                t.wait(("xTe", NT - 1))
                t.transpose(trp_b[0:16, 0:128], smb[:], idn[:])
                t.mark(("kmT",))
                v.wait(("kmT",))
                v.reduce_max(sm[0:16, C_MQ:C_MQ + 1], trp_b[0:16, 0:128],
                             axis=AX.X)
                v.mark(("km16",))
                v.wait(("km16",))
                v.tensor_copy(smb[0:16, 0:1], sm[0:16, C_MQ:C_MQ + 1])
                v.mark(("km16b",))
                t.wait(("km16b",))
                t.wait(("kte", NT - 1))
                t.transpose(trq_b[0:1, 0:16], smb[0:16, 0:1], idn[0:16, 0:16])
                t.mark(("kmT2",))
                v.wait(("kmT2",))
                v.reduce_max(mk2[:], trq_b[0:1, 0:16], axis=AX.X)
                v.mark(("mk2",))

            _emit(nc, "phase1", phase1, P)

        nc.all_engine_barrier()

        with ExitStack() as ctx:
            def sb(name, shape, dtype):
                return ctx.enter_context(nc.sbuf_tensor(name, shape, dtype))

            wq = sb("wq", [128, 8, D], BF16)
            wo = sb("wo", [128, 8, D], BF16)
            QT = sb("QT", [128, 9, 512], BF16)
            xb2 = sb("xb2", [128, 2, D], BF16)
            xfy = sb("xfy", [128, 2, D], F32)
            xTs4 = sb("xTs4", [128, 8, 512], BF16)
            aTs = sb("aTs", [128, 2, 8, 128], BF16)
            p_sb = sb("p_sb", [128, 2, 1024], BF16)
            attO = sb("attO", [128, 4, D], BF16)
            recs = sb("recs", [128, 8], F32)
            negC2 = sb("negC2", [128, 2], F32)
            qmax = sb("qmax", [16, 1], F32)

            sc0 = ctx.enter_context(nc.psum_tensor("sc0", [128, 1024], F32))
            sc1 = ctx.enter_context(nc.psum_tensor("sc1", [128, 1024], F32))
            pvA = ctx.enter_context(nc.psum_tensor("pvA", [128, 4, 66], F32))
            pvB = ctx.enter_context(nc.psum_tensor("pvB", [128, 4, 66], F32))
            mmp = ctx.enter_context(nc.psum_tensor("mmp", [128, 1024], F32))
            mmp_b = mmp.bitcast(BF16)
            sc0_b = sc0.bitcast(BF16)
            scs = (sc0, sc1)
            pv_even = ((pvA, 0, 264), (pvB, 0, 264))
            pv_odd = pv_even

            def phase2(E):
                t, v, a, g, y = E.t, E.v, E.a, E.g, E.y
                E.gdma(("wq",), wq[:],
                       wqkv_hbm[:, 0:D].rearrange("(c p) n -> p c n", p=128))
                E.gdma(("wo",), wo[:],
                       wout_hbm.rearrange("(c p) n -> p c n", p=128))

                st2 = {"mm6": None, "mm7": None}
                chunk_order = {}

                def qsl(cc, i):
                    return (8 * cc + i) % 9

                def emit_qtmm(cc, dch):
                    dc, half = dch // 2, dch % 2
                    if half == 0:
                        t.wait(("qxTe", 4 * cc + 3))
                        t.wait(("wq",))
                        t.wait(st2["mm6"])
                        for ch in range(4):
                            t.matmul(mmp[:, 0:512],
                                     wq[:, ch, 128 * dc:128 * dc + 128],
                                     xTs4[:, ch, :],
                                     start=(ch == 0), stop=False)
                        return
                    for ch in range(4, 8):
                        t.matmul(mmp[:, 0:512],
                                 wq[:, ch, 128 * dc:128 * dc + 128],
                                 xTs4[:, ch, :],
                                 start=False, stop=(ch == 7))
                    t.mark(("qmm8", cc, dc))

                def emit_qev(cc, dc):
                    v.wait(("qmm8", cc, dc))
                    if cc >= 1 and dc >= 1:
                        v.wait(("smm", cc - 1, dc - 1,
                                chunk_order[cc - 1][-1]))
                    v.tensor_copy(QT[:, qsl(cc, dc), :], mmp[:, 0:512])
                    v.mark(("qev", cc, dc))
                    st2["mm6"] = ("qev", cc, dc)

                def emit_qnrm(cc, dc):
                    if dc == 0 and cc >= 1:
                        v.wait(("negC", cc - 1))
                    v.wait(("qev", cc, dc))
                    v.scalar_tensor_tensor(
                        xb2[:, dc % 2, 0:512], QT[:, qsl(cc, dc), :], 1.0,
                        QT[:, qsl(cc, dc), :], ALU.mult, ALU.mult)
                    v.mark(("sqb", cc, dc))
                    t.wait(("sqb", cc, dc))
                    t.wait(st2["mm7"])
                    t.matmul(mmp[:, 512:1024][0:16, 0:512],
                             qpat_s[:, dc, :], xb2[:, dc % 2, 0:512],
                             start=True, stop=True)
                    t.mark(("qnm2", cc, dc))
                    v.wait(("qnm2", cc, dc))
                    if dc == 0:
                        v.memset(qmax[:], 0.0)
                    v.reduce_max(sm[0:16, C_MQ:C_MQ + 1],
                                 mmp[:, 512:1024][0:16, 0:512], axis=AX.X)
                    v.mark(("qnrr", cc, dc))
                    v.wait(("qnrr", cc, dc))
                    v.tensor_max(qmax[:], qmax[:], sm[0:16, C_MQ:C_MQ + 1])
                    v.mark(("qnmx", cc, dc))
                    st2["mm7"] = ("qnmx", cc, dc)

                def emit_qtrans(cc, tt):
                    ti = 4 * cc + tt
                    if ti >= 2:
                        g.wait(("qxT", ti - 2))
                    if tt < 2:
                        g.wait(("qnm2", cc - 1, 6 + tt))
                    E.gdma(("xb2", ti), xb2[:, ti % 2, :],
                           x_hbm[128 * ti:128 * ti + 128, :])
                    mk = "mm6" if tt % 2 == 0 else "mm7"
                    mo = 1024 * (tt % 2)
                    t.wait(("xb2", ti))
                    t.wait(("c", 6))
                    t.wait(st2[mk])
                    for ch in range(8):
                        t.transpose(mmp_b[:, mo + 128 * ch:
                                          mo + 128 * ch + 128],
                                    xb2[:, ti % 2, 128 * ch:128 * ch + 128],
                                    idn[:])
                    t.mark(("qxT", ti))
                    v.wait(("qxT", ti))
                    if tt == 0:
                        v.wait(("qmm8", cc - 1, 7))
                    v.tensor_copy(xTs4[:, :, 128 * tt:128 * tt + 128],
                                  mmp_b[:, mo:mo + 1024].rearrange(
                                      "p (c n) -> p c n", n=128))
                    v.mark(("qxTe", ti))
                    st2[mk] = ("qxTe", ti)

                def emit_proj_trans(cc, tt):
                    ti = 4 * cc + tt
                    mk = "mm6" if tt % 2 == 0 else "mm7"
                    mo = 1024 * (tt % 2)
                    t.wait(("pve", cc, 7))
                    t.wait(st2[mk])
                    for ch in range(8):
                        t.transpose(mmp_b[:, mo + 128 * ch:
                                          mo + 128 * ch + 128],
                                    attO[:, tt, 128 * ch:128 * ch + 128],
                                    idn[:])
                    t.mark(("aT", ti))
                    v.wait(("aT", ti))
                    if ti >= 2:
                        v.wait(("pmm", ti - 2))
                    v.tensor_copy(aTs[:, ti % 2],
                                  mmp_b[:, mo:mo + 1024].rearrange(
                                      "p (c n) -> p c n", n=128))
                    v.mark(("aTe", ti))
                    st2[mk] = ("aTe", ti)

                def emit_proj_mm_a(cc, tt):
                    ti = 4 * cc + tt
                    t.wait(("aTe", ti))
                    t.wait(st2["mm6"])
                    t.wait(("wo",))
                    if ti >= 1:
                        t.wait(("y", ti - 1))
                    for ch in range(8):
                        t.matmul(mmp[:, 0:512],
                                 aTs[:, ti % 2, ch, :],
                                 wo[:, ch, 0:512],
                                 start=(ch == 0), stop=(ch == 7))

                def emit_proj_mm(cc, tt):
                    ti = 4 * cc + tt
                    t.wait(st2["mm7"])
                    for ch in range(8):
                        t.matmul(mmp[:, 512:1024],
                                 aTs[:, ti % 2, ch, :],
                                 wo[:, ch, 512:1024],
                                 start=(ch == 0), stop=(ch == 7))
                    t.mark(("pmm", ti))
                    st2["mm7"] = ("y", ti)
                    st2["mm6"] = ("y", ti)
                    v.wait(("pmm", ti))
                    v.wait(("xf", ti))
                    v.scalar_tensor_tensor(
                        xfy[:, ti % 2, :], mmp[:, 0:1024], 1.0,
                        xfy[:, ti % 2, :], ALU.mult, ALU.add)
                    v.mark(("y", ti))
                    pend_ln.append((emit_proj_ln, cc, tt))

                def emit_proj_ln(cc, tt):
                    ti = 4 * cc + tt
                    v.wait(("y", ti))
                    v.bn_stats(bns[:, 0:6], xfy[:, ti % 2, 0:512])
                    v.bn_stats(bns[:, 6:12], xfy[:, ti % 2, 512:1024])
                    v.mark(("ybn", ti))
                    v.wait(("ybn", ti))
                    v.bn_aggr(bns[:, 12:14], bns[:, 0:12])
                    v.mark(("yag", ti))
                    pend_ln2.append((emit_proj_ln2, cc, tt))

                def emit_proj_ln2(cc, tt):
                    ti = 4 * cc + tt
                    a.wait(("yag", ti))
                    a.activation(smc(C_LV), bns[:, 13:14], AF.Ln,
                                 bias=epsln_s[:])
                    a.mark(("lv", ti))
                    a.activation(smc(C_S1), bns[:, 13:14], AF.Ln)
                    a.mark(("lv0", ti))
                    a.wait(("lv", ti))
                    a.activation(smc(C_RSTD), smc(C_LV), AF.Exp,
                                 scale=-0.5)
                    a.mark(("rstd", ti))
                    a.wait(("lv0", ti))
                    a.activation(smc(C_S1), smc(C_S1), AF.Exp, scale=0.5)
                    a.mark(("e0", ti))
                    v.wait(("rstd", ti))
                    v.tensor_scalar(xfy[:, ti % 2, :], xfy[:, ti % 2, :],
                                    bns[:, 12:13], smc(C_RSTD),
                                    ALU.subtract, ALU.mult)
                    v.mark(("x1a", ti))
                    y.wait(("x1a", ti))
                    E.ydma(("spill", ti),
                           x1_hbm[128 * ti:128 * ti + 128, :],
                           xfy[:, ti % 2, :])
                    if ti + 2 < NT:
                        y.wait(("spill", ti))
                        E.ydma(("xf", ti + 2), xfy[:, ti % 2, :],
                               x_hbm[128 * (ti + 2):128 * (ti + 2) + 128,
                                     :])
                    v.wait(("e0", ti))
                    v.tensor_mul(smc(C_SS1), smc(C_S1), smc(C_RSTD))
                    v.mark(("s1", ti))
                    v.wait(("s1", ti))
                    v.tensor_scalar(smc(C_T1), smc(C_SS1), 1.0,
                                    EPS_RMS, ALU.mult, ALU.add)
                    v.mark(("t1", ti))
                    v.wait(("t1", ti))
                    v.reciprocal(r1_all[:, ti:ti + 1], smc(C_T1))
                    v.mark(("r1", ti))

                pend_in = []
                pend_end = []
                pend_ln = []
                pend_ln2 = []

                def force_ln(ti):
                    def tkey(q):
                        return 4 * q[0][1] + q[0][2] if q else 10 ** 9
                    while min(tkey(pend_ln), tkey(pend_ln2)) <= ti - 2:
                        q = pend_ln2 if tkey(pend_ln2) <= tkey(pend_ln) \
                            else pend_ln
                        f2, c2, t2 = q.pop(0)
                        f2(c2, t2)

                def drain_in():
                    if pend_in:
                        fn, cc, tt = pend_in.pop(0)
                        if fn is emit_proj_mm:
                            force_ln(4 * cc + tt)
                        fn(cc, tt)

                def drain_end(p, budget=1):
                    n = 0
                    while pend_end and pend_end[0][3] <= p and n < budget:
                        fn, cc, tt, _ = pend_end.pop(0)
                        if fn is emit_proj_mm:
                            force_ln(4 * cc + tt)
                        fn(cc, tt)
                        n += 1

                for c in range(NCH):
                    nj = 4 * c + 4
                    if c == 0:
                        for tt in range(4):
                            ti = tt
                            if ti >= 2:
                                g.wait(("qxT", ti - 2))
                            E.gdma(("xb2", ti), xb2[:, ti % 2, :],
                                   x_hbm[128 * ti:128 * ti + 128, :])
                            mk = "mm6" if tt % 2 == 0 else "mm7"
                            mo = 1024 * (tt % 2)
                            t.wait(("xb2", ti))
                            t.wait(("c", 6))
                            t.wait(st2[mk])
                            for ch in range(8):
                                t.transpose(
                                    mmp_b[:, mo + 128 * ch:
                                          mo + 128 * ch + 128],
                                    xb2[:, ti % 2, 128 * ch:128 * ch + 128],
                                    idn[:])
                            t.mark(("qxT", ti))
                            v.wait(("qxT", ti))
                            v.tensor_copy(
                                xTs4[:, :, 128 * tt:128 * tt + 128],
                                mmp_b[:, mo:mo + 1024].rearrange(
                                    "p (c n) -> p c n", n=128))
                            v.mark(("qxTe", ti))
                            st2[mk] = ("qxTe", ti)
                    def emit_cbound(cc, tt=0):
                        v.wait(("qnmx", cc, 7))
                        v.tensor_copy(smb[0:16, 0:1], qmax[:])
                        v.mark(("qm16b", cc))
                        t.wait(("qm16b", cc))
                        t.wait(st2["mm7"])
                        t.transpose(mmp_b[0:1, 1536:1552], smb[0:16, 0:1],
                                    idn[0:16, 0:16])
                        t.mark(("qmT2", cc))
                        v.wait(("qmT2", cc))
                        v.reduce_max(sm[0:1, C_MQ:C_MQ + 1],
                                     mmp_b[0:1, 1536:1552], axis=AX.X)
                        v.mark(("mq2", cc))
                        v.wait(("mq2", cc))
                        v.wait(("mk2",))
                        v.tensor_mul(sm[0:1, C_C2:C_C2 + 1],
                                     sm[0:1, C_MQ:C_MQ + 1], mk2[:])
                        v.mark(("c2", cc))
                        a.wait(("c2", cc))
                        a.activation(sm[0:1, C_LV:C_LV + 1],
                                     sm[0:1, C_C2:C_C2 + 1], AF.Ln)
                        a.mark(("clv", cc))
                        a.wait(("clv", cc))
                        a.activation(sm[0:1, C_CS:C_CS + 1],
                                     sm[0:1, C_LV:C_LV + 1], AF.Exp,
                                     scale=0.5)
                        a.mark(("cs", cc))
                        t.wait(("cs", cc))
                        t.matmul(mmp[:, 544:545], ones_r[:],
                                 sm[0:1, C_CS:C_CS + 1],
                                 start=True, stop=True)
                        t.mark(("cb", cc))
                        if cc >= 2:
                            v.wait(("exp", cc - 2, 7,
                                    chunk_order[cc - 2][-1]))
                        v.wait(("cb", cc))
                        v.tensor_scalar(negC2[:, cc % 2:cc % 2 + 1],
                                        mmp[:, 544:545], -0.125, None,
                                        ALU.mult)
                        v.mark(("negC", cc))
                        st2["mm7"] = ("negC", cc)

                    if c == 0:
                        E.ydma(("xf", 0), xfy[:, 0, :], x_hbm[0:128, :])
                        E.ydma(("xf", 1), xfy[:, 1, :], x_hbm[128:256, :])
                        for dch in range(16):
                            emit_qtmm(0, dch)
                            if dch % 2 == 1:
                                emit_qev(0, dch // 2)
                                emit_qnrm(0, dch // 2)
                        emit_cbound(0)
                    if c + 1 < NCH:
                        pend_end += [(emit_qtrans, c + 1, tt, 0)
                                     for tt in range(4)]
                        for dc in range(8):
                            mp = max(0, dc - 1)
                            pend_end += [(emit_qtmm, c + 1, 2 * dc, mp),
                                         (emit_qtmm, c + 1, 2 * dc + 1, mp),
                                         (emit_qev, c + 1, dc, mp),
                                         (emit_qnrm, c + 1, dc, mp)]
                        pend_end += [(emit_cbound, c + 1, 0, 6)]

                    norder = list(range(nj))
                    chunk_order[c] = norder
                    pos_of = {jj: k for k, jj in enumerate(norder)}
                    start_pos = {s: min(pos_of[jj]
                                        for jj in range(4 * c + s + 1))
                                 for s in range(4)}
                    stop_pos = {s: max(pos_of[jj]
                                       for jj in range(4 * c + s + 1))
                                for s in range(4)}

                    def prev_key(p, pos):
                        if pos >= 2:
                            return (c, p, norder[pos - 2])
                        if p >= 1:
                            return (c, p - 1, norder[nj - 2 + pos])
                        if c >= 1:
                            po = chunk_order[c - 1]
                            return (c - 1, 7, po[len(po) - 2 + pos])
                        return None

                    if True:
                        def emit_scores(p, pos):
                            j = norder[pos]
                            slot = pos % 2
                            r = j - 4 * c
                            off = 128 * r if r > 0 else 0
                            nvalid = 512 - off
                            scp = scs[slot]
                            t.wait(("qev", c, 7))
                            pk = prev_key(p, pos)
                            if pk is not None:
                                t.wait(("exp",) + pk)
                            t.matmul(scp[:, off:512],
                                     KT[0:64, p, 128 * j:128 * j + 128],
                                     QT[0:64, qsl(c, p), off:512],
                                     start=True, stop=True,
                                     tile_position=(0, 0))
                            t.matmul(scp[:, 512 + off:1024],
                                     KT[64:128, p, 128 * j:128 * j + 128],
                                     QT[64:128, qsl(c, p), off:512],
                                     start=True, stop=True,
                                     tile_position=(64, 0))
                            t.mark(("smm", c, p, j))
                            a.wait(("smm", c, p, j))
                            if pk is not None:
                                a.wait(("pv",) + pk)
                            a.wait(("negC", c))
                            a.activation(
                                bass.AP(p_sb, 1024 * slot + off,
                                        [[2048, 128], [512, 2],
                                         [1, nvalid]]),
                                bass.AP(scp, off,
                                        [[1024, 128], [512, 2],
                                         [1, nvalid]]),
                                AF.Exp, bias=negC2[:, c % 2:c % 2 + 1],
                                scale=0.125)
                            a.mark(("exp", c, p, j))
                            if r >= 0:
                                e_m = g if p == 0 else v
                                e_m.wait(("exp", c, p, j))
                                for h in range(2):
                                    mp = bass.AP(p_sb,
                                                 1024 * slot + 512 * h + off,
                                                 [[2048, 128], [1, 128]])
                                    if p == 0:
                                        e_m.tensor_tensor(mp, mp, tri_s[:],
                                                          ALU.mult)
                                    else:
                                        e_m.tensor_mul(mp, mp, tri_s[:])
                                e_m.mark(("msk", c, p, j))

                        def emit_pv(p, pos):
                            j = norder[pos]
                            slot = pos % 2
                            r = j - 4 * c
                            if r >= 0:
                                t.wait(("msk", c, p, j))
                            else:
                                t.wait(("exp", c, p, j))
                            for h in range(2):
                                if pos == 0:
                                    if p >= 1:
                                        t.wait(("pveh", c, p - 1, h))
                                    elif c >= 1:
                                        t.wait(("pveh", c - 1, 7, h))
                                ten, base, pstr = pv_even[h]
                                for s in range(max(0, r), 4):
                                    t.matmul(
                                        bass.AP(ten, base + 66 * s,
                                                [[pstr, 128], [1, 65]]),
                                        p_sb[:, slot,
                                             512 * h + 128 * s:
                                             512 * h + 128 * s + 128],
                                        VS[:, j, 2 * p + h, 0:65],
                                        start=(pos == 0 and s == 0),
                                        stop=(pos == stop_pos[s]),
                                        skip_group_check=True)
                            t.mark(("pv", c, p, j))

                        def emit_epilogue(p):
                            v.wait(("pv", c, p, norder[-1]))
                            if c >= 1:
                                v.wait(("aT", 4 * (c - 1) + 3))
                            for h in range(2):
                                ten, base, pstr = pv_even[h]
                                v.reciprocal(recs[:, 4 * h:4 * h + 4],
                                             bass.AP(ten, base + 64,
                                                     [[pstr, 128], [66, 4]]))
                                v.mark(("rc", c, p, h))
                                v.wait(("rc", c, p, h))
                                for s in range(4):
                                    v.tensor_scalar(
                                        attO[:, s, 128 * p + 64 * h:
                                             128 * p + 64 * h + 64],
                                        bass.AP(ten, base + 66 * s,
                                                [[pstr, 128], [1, 64]]),
                                        recs[:, 4 * h + s:4 * h + s + 1],
                                        None, ALU.mult)
                                v.mark(("pveh", c, p, h))
                            v.mark(("pve", c, p), inc=0)
                            while (pend_end and pend_end[0][0] is emit_qev
                                   and pend_end[0][3] <= p):
                                fn, cc2, tt2, _ = pend_end.pop(0)
                                fn(cc2, tt2)
                            while pend_in:
                                drain_in()

                        seq = [(p, k) for p in range(8) for k in range(nj)]
                        for gx, (p, k) in enumerate(seq):
                            if k == 2 and pend_ln:
                                fn, cc2, tt2 = pend_ln.pop(0)
                                fn(cc2, tt2)
                            if k == 4 and pend_ln2:
                                fn, cc2, tt2 = pend_ln2.pop(0)
                                fn(cc2, tt2)
                            if p == 0 and k >= 2:
                                drain_in()
                                if nj <= 8:
                                    drain_in()
                            elif p >= 1 and k >= 2 and (k % 2 == 0
                                                        or nj <= 16):
                                drain_end(p - 1,
                                          budget=(2 if nj <= 12 else 1))
                            emit_scores(p, k)
                            if gx >= 2:
                                p2, k2 = seq[gx - 2]
                                emit_pv(p2, k2)
                                if k2 == nj - 1:
                                    emit_epilogue(p2)
                        for gx in (len(seq) - 2, len(seq) - 1):
                            p2, k2 = seq[gx]
                            emit_pv(p2, k2)
                            if k2 == nj - 1:
                                emit_epilogue(p2)
                    while pend_end:
                        drain_end(99)
                    pend_in = [(emit_proj_trans, c, 0),
                               (emit_proj_trans, c, 1),
                               (emit_proj_mm_a, c, 0),
                               (emit_proj_mm, c, 0),
                               (emit_proj_trans, c, 2),
                               (emit_proj_mm_a, c, 1),
                               (emit_proj_mm, c, 1),
                               (emit_proj_trans, c, 3)]
                    pend_end = [(emit_proj_mm_a, c, 2, 0),
                                (emit_proj_mm, c, 2, 0),
                                (emit_proj_mm_a, c, 3, 1),
                                (emit_proj_mm, c, 3, 1)]

                while pend_in:
                    drain_in()
                while pend_end or pend_ln or pend_ln2:
                    while pend_ln or pend_ln2:
                        if pend_ln:
                            fn, cc2, tt2 = pend_ln.pop(0)
                            fn(cc2, tt2)
                        while pend_ln2:
                            fn, cc2, tt2 = pend_ln2.pop(0)
                            fn(cc2, tt2)
                    if pend_end:
                        drain_end(99)

                E.y.wait(("spill", NT - 1))
                E.y.wait(("spill", NT - 2))

            def trx_slice(mb):
                return mb[0:1, 512:528]

            _emit(nc, "phase2", phase2, P)

        nc.all_engine_barrier()
        mctx.close()

        with ExitStack() as ctx:
            def sb(name, shape, dtype):
                return ctx.enter_context(nc.sbuf_tensor(name, shape, dtype))

            wupT = sb("wupT", [128, 8, FF], BF16)
            wdnT = sb("wdnT", [128, 32, D], BF16)
            wr3 = sb("wr3", [128, 3, D], F32)
            m1b = sb("m1b", [128, 2, D], BF16)
            m2b = sb("m2b", [128, 1, D], BF16)
            absw = sb("absw", [128, D], F32)
            wtb = sb("wtb", [128, 2, D], BF16)
            x1b = sb("x1b", [128, 2, D], BF16)
            x1f = sb("x1f", [128, 2, D], F32)
            x1Ts = sb("x1Ts", [128, 2, 8, 128], BF16)
            h_sb = sb("h_sb", [128, 2, FF], BF16)
            hTs = sb("hTs", [128, 32, 128], BF16)
            outs = sb("outs", [128, 2, D], F32)
            sqd3 = sb("sqd3", [128, 2048], BF16)

            upq = [ctx.enter_context(
                nc.psum_tensor(f"upq{i}", [128, 512], F32))
                for i in range(2)]
            tra = ctx.enter_context(nc.psum_tensor("tra", [128, 512], F32))
            trb = ctx.enter_context(nc.psum_tensor("trb", [128, 512], F32))
            dwn2 = [ctx.enter_context(
                nc.psum_tensor(f"dwn{i}", [128, 1024], F32))
                for i in range(2)]
            tra_b = tra.bitcast(BF16)
            trb_b = trb.bitcast(BF16)

            def phase3(E):
                t, v, a, g, y = E.t, E.v, E.a, E.g, E.y
                a.activation(smc(C_SD), ones_c[:], AF.Gelu)
                a.mark(("apin3",))

                state = {"tra": None, "upq": [None, None]}

                def tern_chunk(r):
                    if r >= 3:
                        y.wait(("tern", r - 3))
                    if r < 32:
                        src = wup_hbm[128 * r:128 * r + 128, :]
                        thr = thr4[:, 0:1]
                    else:
                        rr = r - 32
                        qq, dc = rr // 8, rr % 8
                        src = wdn_hbm[128 * dc:128 * dc + 128,
                                      1024 * qq:1024 * qq + 1024]
                        thr = thr4[:, 1:2]
                    E.ydma(("w3", r), wr3[:, r % 3, :], src)
                    a.wait(("w3", r))
                    a.wait(("apin3",))
                    if r >= 2:
                        a.wait(("tern", r - 2))
                    a.activation(m1b[:, r % 2, :], wr3[:, r % 3, :], AF.Sign)
                    a.mark(("sgn", r))
                    if r >= 1:
                        a.wait(("mskt", r - 1))
                    a.activation(absw[:], wr3[:, r % 3, :], AF.Abs)
                    a.mark(("absw", r))
                    v.wait(("absw", r))
                    if r >= 2:
                        v.wait(("tern", r - 2))
                    v.tensor_scalar(m2b[:, 0, :], absw[:],
                                    thr, None, ALU.is_gt)
                    v.mark(("mskt", r))
                    v.wait(("mskt", r))
                    v.wait(("sgn", r))
                    if r >= 2:
                        v.wait(("wTr", r - 2))
                    v.tensor_mul(wtb[:, r % 2, :], m1b[:, r % 2, :],
                                 m2b[:, 0, :])
                    v.mark(("tern", r))
                    t.wait(("tern", r))
                    t.wait(state["tra"])
                    for ch in range(8):
                        t.transpose(tra_b[:, 128 * ch:128 * ch + 128],
                                    wtb[:, r % 2, 128 * ch:128 * ch + 128],
                                    idn[:])
                    t.mark(("wTr", r))
                    v.wait(("wTr", r))
                    if r < 32:
                        dst = wupT[:, :, 128 * r:128 * r + 128]
                    else:
                        rr = r - 32
                        qq, dc = rr // 8, rr % 8
                        dst = wdnT[:, 8 * qq:8 * qq + 8,
                                   128 * dc:128 * dc + 128]
                    v.tensor_copy(dst, tra_b[:, 0:1024].rearrange(
                        "p (c n) -> p c n", n=128))
                    v.mark(("wTe", r))
                    state["tra"] = ("wTe", r)

                upq_hist = []

                def issue_x1b(i):
                    g.wait(("spill", i))
                    if i >= 2:
                        g.wait(("x1T", i - 2))
                    E.gdma(("x1b", i), x1b[:, i % 2, :],
                           x1_hbm[128 * i:128 * i + 128, :])

                def up_trans(i):
                    t.wait(("x1b", i))
                    t.wait(state["tra"])
                    for ch in range(8):
                        t.transpose(tra_b[:, 128 * ch:128 * ch + 128],
                                    x1b[:, i % 2, 128 * ch:128 * ch + 128],
                                    idn[:])
                    t.mark(("x1T", i))
                    v.wait(("x1T", i))
                    if i >= 2:
                        v.wait(("up", i - 2, 7))
                    v.tensor_copy(x1Ts[:, i % 2], tra_b[:, 0:1024].rearrange(
                        "p (c n) -> p c n", n=128))
                    v.mark(("x1Te", i))
                    state["tra"] = ("x1Te", i)
                    if i >= 2:
                        y.wait(("o", i - 2))
                    y.wait(("spill", i))
                    E.ydma(("x1f", i), x1f[:, i % 2, :],
                           x1_hbm[128 * i:128 * i + 128, :])

                def up_part_q(i, q):
                    if q == 0:
                        t.wait(("x1Te", i))
                    t.wait(("wTe", 4 * q + 3))
                    bank = len(upq_hist) % 2
                    if len(upq_hist) >= 2:
                        t.wait(("gel",) + upq_hist[-2])
                    upq_hist.append((i, q))
                    for ch in range(8):
                        t.matmul(upq[bank][:, 0:512],
                                 x1Ts[:, i % 2, ch, :],
                                 wupT[:, ch, 512 * q:512 * q + 512],
                                 start=(ch == 0), stop=(ch == 7))
                    t.mark(("up", i, q))
                    a.wait(("up", i, q))
                    if i >= 2:
                        a.wait(("hT", i - 2, q // 2))
                    a.activation(h_sb[:, i % 2, 512 * q:512 * q + 512],
                                 upq[bank][:, 0:512], AF.Gelu,
                                 scale=r1_all[:, i:i + 1])
                    a.mark(("gel", i, q))

                def up_part(i):
                    for q in range(8):
                        up_part_q(i, q)

                def down_part(i):
                    def hround(k):
                        t.wait(("gel", i, 2 * k + 1))
                        bb = (trb_b, tra_b)[k % 2]
                        if k % 2 == 0:
                            if i >= 1 or k >= 2:
                                pk = (i, k - 2) if k >= 2 else (i - 1, 2)
                                t.wait(("hTe",) + pk)
                        else:
                            t.wait(state["tra"])
                        for u in range(8):
                            t.transpose(bb[:, 128 * u:128 * u + 128],
                                        h_sb[:, i % 2,
                                             1024 * k + 128 * u:
                                             1024 * k + 128 * u + 128],
                                        idn[:])
                        t.mark(("hT", i, k))
                        ev = a if k % 2 == 0 else v
                        ev.wait(("hT", i, k))
                        ev_src = bb[:, 0:1024].rearrange(
                            "p (c n) -> p c n", n=128)
                        if k % 2 == 0:
                            ev.activation(hTs[:, 8 * k:8 * k + 8, :],
                                          ev_src, AF.Copy)
                        else:
                            ev.tensor_copy(hTs[:, 8 * k:8 * k + 8, :],
                                           ev_src)
                        ev.mark(("hTe", i, k))
                        if k % 2 == 1:
                            state["tra"] = ("hTe", i, k)

                    def dround(k):
                        if k == 0 and i >= 2:
                            t.wait(("y2", i - 2))
                        t.wait(("hTe", i, k))
                        t.wait(("wTe", 32 + 8 * k + 7))
                        for fc in range(8 * k, 8 * k + 8):
                            for nb in range(2):
                                t.matmul(
                                    dwn2[i % 2][:, 512 * nb:512 * nb + 512],
                                    hTs[:, fc, :],
                                    wdnT[:, fc, 512 * nb:512 * nb + 512],
                                    start=(fc == 0), stop=(fc == 31))

                    hround(0)
                    hround(1)
                    dround(0)
                    hround(2)
                    dround(1)
                    hround(3)
                    dround(2)
                    dround(3)
                    t.mark(("dn", i))
                    a.wait(("gel", i, 7))
                    if i >= 1:
                        a.wait(("y2ss", i - 1))
                    a.activation(sqd3[:], h_sb[:, i % 2, 0:2048], AF.Square,
                                 accum_out=smc(C_SS2))
                    a.mark(("hssa", i))
                    a.wait(("hssa", i))
                    a.activation(sqd3[:], h_sb[:, i % 2, 2048:4096],
                                 AF.Square, accum_out=sm[:, 22:23])
                    a.mark(("hss", i))
                    v.wait(("hss", i))
                    v.tensor_tensor(smc(C_SS2), smc(C_SS2), sm[:, 22:23],
                                    ALU.add)
                    v.mark(("hsum", i))
                    g.wait(("hsum", i))
                    g.tensor_tensor(smc(C_S2), smc(C_SS2), half_s[:],
                                    ALU.pow)
                    g.mark(("s2", i))
                    v.wait(("s2", i))
                    v.tensor_scalar(smc(C_T1), smc(C_S2), 1.0 / 64.0,
                                    EPS_RMS, ALU.mult, ALU.add)
                    v.mark(("t2", i))
                    v.wait(("t2", i))
                    v.reciprocal(smc(C_R2), smc(C_T1))
                    v.mark(("r2", i))
                    v.wait(("dn", i))
                    v.wait(("x1f", i))
                    v.wait(("r2", i))
                    v.scalar_tensor_tensor(
                        x1f[:, i % 2, :], dwn2[i % 2][:, 0:1024], smc(C_R2),
                        x1f[:, i % 2, :], ALU.mult, ALU.add,
                        accum_out=smc(C_YSUM))
                    v.mark(("y2", i))
                    a.wait(("y2", i))
                    a.activation(sqd3[:, 0:D], x1f[:, i % 2, :], AF.Square,
                                 accum_out=smc(C_YSS))
                    a.mark(("y2ss", i))
                    v.wait(("y2", i))
                    v.tensor_scalar(smc(C_M), smc(C_YSUM), 1.0 / D, None,
                                    ALU.mult)
                    v.mark(("m2", i))
                    a.wait(("m2", i))
                    a.activation(smc(C_MSQ), smc(C_M), AF.Square)
                    a.mark(("msq2", i))
                    v.wait(("y2ss", i))
                    v.wait(("msq2", i))
                    v.tensor_scalar(smc(C_VAR), smc(C_YSS), 1.0 / D,
                                    smc(C_MSQ), ALU.mult, ALU.subtract)
                    v.mark(("var2", i))
                    v.wait(("var2", i))
                    v.tensor_scalar(smc(C_VE), smc(C_VAR), EPS_LN, None,
                                    ALU.add)
                    v.mark(("ve2", i))
                    g.wait(("ve2", i))
                    g.tensor_tensor(smc(C_SD), smc(C_VE), half_s[:],
                                    ALU.pow)
                    g.mark(("sd2", i))
                    v.wait(("sd2", i))
                    v.reciprocal(smc(C_RSTD), smc(C_SD))
                    v.mark(("rstd2", i))
                    v.wait(("rstd2", i))
                    if i >= 2:
                        v.wait(("oD", i - 2))
                    v.tensor_scalar(outs[:, i % 2, :], x1f[:, i % 2, :],
                                    smc(C_M), smc(C_RSTD), ALU.subtract,
                                    ALU.mult)
                    v.mark(("o", i))
                    y.wait(("o", i))
                    E.ydma(("oD", i), out_hbm[128 * i:128 * i + 128, :],
                           outs[:, i % 2, :])

                for r in range(32):
                    tern_chunk(r)
                issue_x1b(0)
                up_trans(0)
                up_part(0)
                issue_x1b(1)
                up_trans(1)
                up_part(1)
                issue_x1b(2)
                for r in range(32, 64):
                    tern_chunk(r)
                for i in range(NT):
                    if i + 3 < NT:
                        issue_x1b(i + 3)
                    if i + 2 < NT:
                        up_trans(i + 2)
                    down_part(i)
                    if i + 2 < NT:
                        up_part(i + 2)
                y.wait(("oD", NT - 2))
                y.wait(("oD", NT - 1))

            _emit(nc, "phase3", phase3, P)

    return nc


_NC_CACHE = {}


def kernel(L0_attn_x, L0_attn_w_qkv, L0_attn_w_out, L0_ffn_w_up,
           L0_ffn_w_down):
    x = np.ascontiguousarray(np.asarray(L0_attn_x, dtype=np.float32))
    wqkv = np.ascontiguousarray(np.asarray(L0_attn_w_qkv, dtype=np.float32))
    wout = np.ascontiguousarray(np.asarray(L0_attn_w_out, dtype=np.float32))
    wup = np.ascontiguousarray(np.asarray(L0_ffn_w_up, dtype=np.float32))
    wdn = np.ascontiguousarray(np.asarray(L0_ffn_w_down, dtype=np.float32))
    B = x.shape[0]
    if "nc" not in _NC_CACHE:
        _NC_CACHE["nc"] = build_program()
    nc = _NC_CACHE["nc"]
    in_maps = [{"x": np.ascontiguousarray(x[b]), "w_qkv": wqkv,
                "w_out": wout, "w_up": wup, "w_down": wdn}
               for b in range(B)]
    res = run_bass_kernel_spmd(nc, in_maps, core_ids=list(range(B)))
    out = np.stack([np.asarray(res.results[b]["out"]) for b in range(B)],
                   axis=0)
    return out.astype(np.float32)
